# version 16
# baseline (speedup 1.0000x reference)
import numpy as np
import concourse.bass as bass
import concourse.mybir as mybir
from concourse.bass_utils import run_bass_kernel_spmd
from contextlib import ExitStack

F32 = mybir.dt.float32
BF16 = mybir.dt.bfloat16
F16 = mybir.dt.float16
I16 = mybir.dt.int16
U8 = mybir.dt.uint8
ALU = mybir.AluOpType
AF = mybir.ActivationFunctionType
AX = mybir.AxisListType.X

S = 2048
DM = 4096
D = 128
HPC = 16
SCALE = 1.0 / float(np.sqrt(128.0))
EXPB = -20.0
EPS = 1e-5
VIS_LO, VIS_HI = 1, 513
TXT_LO, TXT_HI = 513, 1992


class Sem:
    __slots__ = ("h", "n", "name")

    def __init__(self, h, name):
        self.h = h
        self.n = 0
        self.name = name


class Prog:
    ENGINES = ("tensor", "vector", "scalar", "gpsimd", "sync")

    def __init__(self, nc, stack):
        self.nc = nc
        self.stack = stack
        self.ops = {e: [] for e in self.ENGINES}
        self.last = {e: None for e in self.ENGINES}
        self.auto_serial = {}
        self._serial_prev = {}

    def sem(self, name):
        return Sem(self.stack.enter_context(self.nc.semaphore(name)), name)

    def op(self, eng, fn, waits=(), inc=None, inck=1, is_dma=False,
           sdep="auto"):
        waits = tuple((s, v) for (s, v) in waits if v is not None)
        ser = self.auto_serial.get(eng)
        if ser is not None:
            if sdep == "auto":
                sp = self._serial_prev.get(eng)
                if sp is not None:
                    waits = waits + (sp,)
            elif sdep is not None:
                waits = waits + ((ser, sdep),)
            if not is_dma and inc is None:
                inc = ser
        ev = None
        if inc is not None:
            inc.n += inck
            ev = inc.n
            self.last[eng] = (inc, ev)
            if ser is not None and not is_dma:
                self._serial_prev[eng] = (inc, ev)
        self.ops[eng].append((fn, waits, inc, inck))
        return ev

    def dma(self, q, qsem, out, in_, waits=(), **kw):
        def fn(e, out=out, in_=in_, kw=kw):
            return e.dma_start(out=out, in_=in_, **kw)
        return self.op(q, fn, waits, inc=qsem, inck=16, is_dma=True)

    def barrier_waits(self):
        return tuple(self.last[e] for e in self.ENGINES if self.last[e])

    def emit_phase(self, block):
        for eng in self.ENGINES:
            ops = self.ops[eng]
            if not ops:
                continue
            getattr(block, eng)(self._mk_body(ops))
        self.ops = {e: [] for e in self.ENGINES}

    def _mk_body(self, ops):
        def body(e):
            waited = {}
            for (fn, waits, inc, inck) in ops:
                for (s, v) in waits:
                    if waited.get(id(s), -1) >= v:
                        continue
                    e.wait_ge(s.h, v)
                    waited[id(s)] = v
                inst = fn(e)
                if inc is not None:
                    inst.then_inc(inc.h, inck)
        return body


def build(dbg=False):
    nc = bass.Bass("TRN2", target_bir_lowering=False,
                   detect_race_conditions=False)

    xT_in = nc.dram_tensor("xT", [DM, S], F32, kind="ExternalInput").ap()
    xres_in = nc.dram_tensor("xres", [1024, DM], F32, kind="ExternalInput").ap()
    qwT_in = nc.dram_tensor("qwT", [DM, 2048], F32, kind="ExternalInput").ap()
    kwT_in = nc.dram_tensor("kwT", [DM, 2048], F32, kind="ExternalInput").ap()
    vwT_in = nc.dram_tensor("vwT", [DM, 2048], F32, kind="ExternalInput").ap()
    owT_in = nc.dram_tensor("owT", [2048, DM], F32, kind="ExternalInput").ap()
    pkT_in = nc.dram_tensor("pkT", [2048, DM], I16, kind="ExternalInput").ap()
    scP_in = nc.dram_tensor("scP", [2048, DM], F16, kind="ExternalInput").ap()
    mkP_in = nc.dram_tensor("mkP", [2048, DM], U8, kind="ExternalInput").ap()
    biasr_in = nc.dram_tensor("biasr", [128, DM], F32, kind="ExternalInput").ap()
    cosD_in = nc.dram_tensor("cosD", [128, S], F32, kind="ExternalInput").ap()
    sinD_in = nc.dram_tensor("sinD", [128, S], F32, kind="ExternalInput").ap()
    flag_in = nc.dram_tensor("flag", [1, 1], F32, kind="ExternalInput").ap()
    out_ext = nc.dram_tensor("out", [1024, DM], F32, kind="ExternalOutput").ap()

    qT_d = nc.dram_tensor("qT_d", [HPC, 128, S], BF16).ap()
    kT_d = nc.dram_tensor("kT_d", [HPC, 128, S], BF16).ap()
    v_d = nc.dram_tensor("v_d", [S, 2048], BF16).ap()
    wdq_d = nc.dram_tensor("wdq_d", [16, 128, 32, 256], BF16).ap()
    rsi_d = nc.dram_tensor("rsi_d", [S, DM], BF16).ap()
    rso_d = nc.dram_tensor("rso_d", [1024, DM], BF16).ap()
    impi_d = nc.dram_tensor("impi_d", [1, 512], F32).ap()
    impo_d = nc.dram_tensor("impo_d", [1, 512], F32).ap()
    mask_d = nc.dram_tensor("mask_d", [1, 768], F32).ap()
    dbgkind = "ExternalOutput" if dbg == 2 else "Internal"
    dbg1_d = nc.dram_tensor("dbg1_d", [128, 8 * DM], BF16, kind=dbgkind).ap() \
        if dbg else None
    dbg2_d = nc.dram_tensor("dbg2_d", [128, 32 * 512], BF16, kind=dbgkind).ap() \
        if dbg else None
    dbg3_d = nc.dram_tensor("dbg3_d", [128, 8 * DM], BF16, kind=dbgkind).ap() \
        if dbg else None
    if dbg == 2:
        dbg_qT = nc.dram_tensor("dbg_qT", [HPC, 128, S], BF16, kind=dbgkind).ap()
        dbg_kT = nc.dram_tensor("dbg_kT", [HPC, 128, S], BF16, kind=dbgkind).ap()
        dbg_v = nc.dram_tensor("dbg_v", [S, 2048], BF16, kind=dbgkind).ap()
        dbg_w = nc.dram_tensor("dbg_w", [16, 128, 32, 256], BF16, kind=dbgkind).ap()
        dbg_rsi = nc.dram_tensor("dbg_rsi", [S, DM], BF16, kind=dbgkind).ap()
        dbg_rso = nc.dram_tensor("dbg_rso", [1024, DM], BF16, kind=dbgkind).ap()
        dbg_msk = nc.dram_tensor("dbg_msk", [1, 768], F32, kind=dbgkind).ap()

    pairs = [[0, 1], [2, 3], [4, 5], [6, 7]]
    allg = [[0, 1, 2, 3, 4, 5, 6, 7]]

    with ExitStack() as stack:
        P = Prog(nc, stack)
        qg = P.sem("qg")
        qs = P.sem("qs")
        qa = P.sem("qa")
        ccol = P.sem("ccol")
        ts = P.sem("ts")
        vs = P.sem("vs")
        ss = P.sem("ss")
        gs = P.sem("gs")
        P.auto_serial = {"vector": vs, "scalar": ss}

        PB = [stack.enter_context(nc.psum_tensor(f"pb{i}", [128, 512], F32))
              for i in range(7)]
        PBt = stack.enter_context(nc.psum_tensor("pbt", [128, 1024], BF16))

        sb = lambda st_, n, shp, dt: st_.enter_context(nc.sbuf_tensor(n, shp, dt))
        ident = sb(stack, "ident", [128, 128], BF16)
        tri = sb(stack, "tri", [128, 128], BF16)
        maskb = sb(stack, "maskb", [128, 8], F32)
        flag_sb = sb(stack, "flag_sb", [1, 1], F32)
        ones_r = sb(stack, "ones_r", [1, 128], F32)
        one1 = sb(stack, "one1", [1, 1], F32)
        stat = sb(stack, "stat", [128, 16], F32)
        b20 = sb(stack, "b20", [128, 1], F32)
        beps = sb(stack, "beps", [128, 1], F32)

        block = stack.enter_context(nc.Block())

        ev_flag = P.dma("sync", qs, flag_sb[:], flag_in[:])
        e1 = P.op("gpsimd", lambda g: g.memset(ident[:], 0.0), inc=gs)
        e2 = P.op("gpsimd", lambda g: g.affine_select(
            out=ident[:], in_=ident[:], compare_op=ALU.not_equal, fill=1.0,
            base=0, pattern=[[-1, 128]], channel_multiplier=1),
            waits=[(gs, e1)], inc=gs)
        e3 = P.op("gpsimd", lambda g: g.memset(tri[:], 1.0),
                  waits=[(gs, e2)], inc=gs)
        ev_tri = P.op("gpsimd", lambda g: g.affine_select(
            out=tri[:], in_=tri[:], compare_op=ALU.is_ge, fill=0.0,
            base=0, pattern=[[1, 128]], channel_multiplier=-1),
            waits=[(gs, e3)], inc=gs)
        P.op("vector", lambda v: v.memset(ones_r[:], 1.0))
        P.op("vector", lambda v: v.memset(b20[:], EXPB))
        P.op("vector", lambda v: v.memset(beps[:], EPS))
        ev_one = P.op("vector", lambda v: v.memset(one1[:], 1.0), inc=vs)
        P.emit_phase(block)

        MASK_READY = None
        with ExitStack() as s1:
            xT = sb(s1, "xT_sb", [128, 32 * S], BF16)
            wb = [sb(s1, f"wb{i}", [128, 32 * 256], BF16) for i in range(2)]
            cos_sb = sb(s1, "cos_sb", [128, 512], F32)
            sin_sb = sb(s1, "sin_sb", [128, 512], F32)
            kraw = sb(s1, "kraw", [128, 1024], F32)
            kp = sb(s1, "kp", [128, 1024], F32)
            robuf = sb(s1, "robuf", [128, 1024], BF16)
            redt = sb(s1, "redt", [128, 1], F32)
            vout = sb(s1, "vout", [128, 256], BF16)
            sq16 = sb(s1, "sq16", [128, 16], F32)
            sq16b = sb(s1, "sq16b", [128, 16], BF16)
            kvisb = sb(s1, "kvisb", [128, 16 * 512], BF16)
            impf = sb(s1, "impf", [1, 512], F32)
            impfh = sb(s1, "impfh", [1, 512], BF16)
            impfl = sb(s1, "impfl", [1, 512], F32)
            impflb = sb(s1, "impflb", [1, 512], BF16)
            ones_rb1 = sb(s1, "ones_rb1", [1, 128], BF16)
            one1b = sb(s1, "one1b", [1, 1], BF16)
            impB = sb(s1, "impB", [128, 512], F32)
            impP = sb(s1, "impP", [128, 4], F32)
            gt512 = sb(s1, "gt512", [128, 512], F32)
            rank4 = sb(s1, "rank4", [128, 4], F32)
            m768 = sb(s1, "m768", [1, 768], F32)

            xTv = xT.rearrange("p (k t) -> p k t", t=S)
            wbv = [w.rearrange("p (k w) -> p k w", w=256) for w in wb]

            P.op("vector", lambda v: v.memset(ones_rb1[:], 1.0))
            P.op("vector", lambda v: v.memset(one1b[:], 1.0))
            ev_sq0 = P.op("vector", lambda v: v.memset(sq16[:], 0.0), inc=vs)

            ev_xt = []
            ev_xt.append(P.dma(
                "gpsimd", qg, xTv[:, :, 0:512],
                xT_in[:, 0:512].rearrange("(k p) t -> p k t", p=128)))

            ev_wload = {}
            proj_evac = []
            ev_t2 = None
            ev_addk = None
            ev_robf = None
            ev_shift = None
            ev_sq = ev_sq0
            last_qs_store = None
            n_unit = 0
            v_unit = 0
            ev_vout_dma = None
            ev_vout_evac = []
            gi = 0

            for pi, wsrc in enumerate([qwT_in, kwT_in, vwT_in]):
                dst = [qT_d, kT_d, None][pi]
                for g in range(8):
                    waits = []
                    if gi >= 2:
                        waits.append((ts, ev_wload[gi - 2]))
                    ev_w = P.dma("gpsimd", qg, wbv[gi % 2][:, :, :],
                                 wsrc[:, 256 * g:256 * (g + 1)].rearrange(
                                     "(k p) w -> p k w", p=128), waits=waits)
                    if gi == 0:
                        for c_ in range(1, 4):
                            ev_xt.append(P.dma(
                                "gpsimd", qg,
                                xTv[:, :, 512 * c_:512 * (c_ + 1)],
                                xT_in[:, 512 * c_:512 * (c_ + 1)].rearrange(
                                    "(k p) t -> p k t", p=128)))
                    mm_ev = None
                    if pi < 2:
                        for c in range(4):
                            cw = [(qg, ev_w), (qg, ev_xt[c])]
                            if ev_t2 is not None:
                                cw.append((vs, ev_t2))
                            evc = P.dma("sync", qs, cos_sb[:],
                                        cosD_in[:, 512 * c:512 * (c + 1)],
                                        waits=cw)
                            evc2 = P.dma("sync", qs, sin_sb[:],
                                         sinD_in[:, 512 * c:512 * (c + 1)])
                            for hh in range(2):
                                bank = PB[n_unit % 2]
                                mw = [(qg, ev_w), (qg, ev_xt[c])]
                                if n_unit >= 2:
                                    mw.append((ss, proj_evac[n_unit - 2]))
                                for k in range(32):
                                    def f(t, bank=bank, k=k, hh=hh, c=c,
                                          wv=wbv[gi % 2]):
                                        return t.matmul(
                                            bank[:],
                                            wv[:, k, 128 * hh:128 * (hh + 1)],
                                            xTv[:, k, 512 * c:512 * (c + 1)],
                                            start=(k == 0), stop=(k == 31))
                                    mm_ev = P.op("tensor", f,
                                                 waits=mw if k == 0 else (),
                                                 inc=ts if k == 31 else None)
                                ew = [(ts, mm_ev)]
                                if ev_addk is not None:
                                    ew.append((vs, ev_addk))
                                if ev_shift is not None:
                                    ew.append((qs, ev_shift))
                                eev = P.op(
                                    "scalar", lambda sc, bank=bank, hh=hh:
                                    sc.copy(kraw[:, 512 * hh:512 * (hh + 1)],
                                            bank[:]), waits=ew, inc=ss,
                                    sdep=None)
                                proj_evac.append(eev)
                                n_unit += 1
                            shw = [(ss, proj_evac[-1])]
                            if ev_robf is not None:
                                shw.append((vs, ev_robf))
                            P.dma("sync", qs, kp[0:64, :], kraw[64:128, :],
                                  waits=shw)
                            ev_shift = P.dma("sync", qs, kp[64:128, :],
                                             kraw[0:64, :])
                            k3 = kraw.rearrange("p (h t) -> p h t", t=512)
                            p3 = kp.rearrange("p (h t) -> p h t", t=512)
                            cb = cos_sb[:].unsqueeze(1).broadcast_to([128, 2, 512])
                            sbx = sin_sb[:].unsqueeze(1).broadcast_to([128, 2, 512])
                            P.op("vector", lambda v, k3=k3, cb=cb:
                                 v.tensor_tensor(k3, k3, cb, ALU.mult),
                                 waits=[(qs, evc2), (qs, ev_shift)])
                            ev_t2 = P.op("vector", lambda v, p3=p3, sbx=sbx:
                                         v.tensor_tensor(p3, p3, sbx, ALU.mult),
                                         inc=vs)
                            rw = []
                            if last_qs_store is not None:
                                rw.append((qs, last_qs_store))
                            r3 = robuf.rearrange("p (h t) -> p h t", t=512)
                            ev_addk = P.op("vector", lambda v, r3=r3, k3=k3, p3=p3:
                                           v.tensor_tensor(r3, k3, p3, ALU.add),
                                           waits=rw, inc=vs)
                            ev_robf = ev_addk
                            if pi == 0:
                                tlo = max(TXT_LO, 512 * c) - 512 * c
                                thi = min(TXT_HI, 512 * (c + 1)) - 512 * c
                                if thi > tlo:
                                    for hh in range(2):
                                        hcol = 2 * g + hh
                                        P.op("vector",
                                             lambda v, hh=hh, tlo=tlo, thi=thi:
                                             v.reduce_sum(
                                                 redt[:],
                                                 robuf[:, 512 * hh + tlo:
                                                       512 * hh + thi], AX))
                                        ev_sq = P.op(
                                            "vector", lambda v, hcol=hcol:
                                            v.tensor_tensor(
                                                sq16[:, hcol:hcol + 1],
                                                sq16[:, hcol:hcol + 1],
                                                redt[:], ALU.add), inc=vs)
                                    ev_robf = ev_sq
                            for hh in range(2):
                                h = 2 * g + hh
                                last_qs_store = P.dma(
                                    "sync", qs,
                                    dst[h, :, 512 * c:512 * (c + 1)],
                                    robuf[:, 512 * hh:512 * (hh + 1)],
                                    waits=[(vs, ev_addk)])
                                if pi == 1 and c == 1:
                                    ev_kvl = P.dma(
                                        "sync", qs,
                                        kvisb[:, 512 * h:512 * (h + 1)],
                                        kT_d[h, :, VIS_LO:VIS_HI])
                    else:
                        for c in range(4):
                            for sub in range(4):
                                tt = 4 * c + sub
                                bank = PB[v_unit % 2]
                                mw = [(qg, ev_w), (qg, ev_xt[c])]
                                if v_unit >= 2:
                                    mw.append((ss, ev_vout_evac[v_unit - 2]))
                                for k in range(32):
                                    def f(t, bank=bank, k=k, tt=tt,
                                          wv=wbv[gi % 2]):
                                        return t.matmul(
                                            bank[:, 0:256],
                                            xTv[:, k, 128 * tt:128 * (tt + 1)],
                                            wv[:, k, :],
                                            start=(k == 0), stop=(k == 31))
                                    mm_ev = P.op("tensor", f,
                                                 waits=mw if k == 0 else (),
                                                 inc=ts if k == 31 else None)
                                ew = [(ts, mm_ev)]
                                if ev_vout_dma is not None:
                                    ew.append((qs, ev_vout_dma))
                                eev = P.op("scalar", lambda sc, bank=bank:
                                           sc.copy(vout[:], bank[:, 0:256]),
                                           waits=ew, inc=ss, sdep=None)
                                ev_vout_evac.append(eev)
                                ev_vout_dma = P.dma(
                                    "sync", qs,
                                    v_d[128 * tt:128 * (tt + 1),
                                        256 * g:256 * (g + 1)],
                                    vout[:], waits=[(ss, eev)])
                                v_unit += 1
                    ev_wload[gi] = mm_ev
                    gi += 1

                if pi == 1:
                    ev_k_st = last_qs_store
                    ev_sqb = P.op("vector",
                                  lambda v: v.tensor_copy(sq16b[:], sq16[:]),
                                  waits=[(vs, ev_sq)], inc=vs)
                    impev = None
                    for h in range(HPC):
                        def f(t, h=h):
                            return t.matmul(
                                PB[4][0:1, :], sq16b[:, h:h + 1],
                                kvisb[:, 512 * h:512 * (h + 1)],
                                start=(h == 0), stop=(h == HPC - 1))
                        w2 = [(qs, ev_kvl), (vs, ev_sqb)] if h == 0 else []
                        impev = P.op("tensor", f, waits=w2,
                                     inc=ts if h == HPC - 1 else None)
                    ev_if = P.op("vector", lambda v: v.tensor_scalar(
                        impf[:], PB[4][0:1, :], flag_sb[0:1, 0:1], None,
                        ALU.mult), waits=[(ts, impev), (qs, ev_flag)], inc=vs,
                        sdep=None)
                    ev_iw = P.dma("sync", qs, impi_d[:], impf[:],
                                  waits=[(vs, ev_if)])
                    ev_ar = P.op("gpsimd", lambda g: g.collective_compute(
                        "AllReduce", ALU.add, replica_groups=allg,
                        ins=[impi_d.opt()], outs=[impo_d.opt()]),
                        waits=[(qs, ev_iw)], inc=ccol)
                    ev_ir = P.dma("gpsimd", qg, impf[:], impo_d[:],
                                  waits=[(ccol, ev_ar)])
                    saved_ev_ir = ev_ir
                    saved_ev_if = ev_if

            if True:
                    ev_ir = saved_ev_ir
                    ev_if = saved_ev_if
                    ev_hi = P.op("vector", lambda v: v.tensor_copy(
                        impfh[:], impf[:]),
                        waits=[(qg, ev_ir), (vs, ev_if)], sdep=None)
                    P.op("vector", lambda v: v.tensor_tensor(
                        impfl[:], impf[:], impfh[:], ALU.subtract))
                    ev_hl2 = P.op("vector", lambda v: v.tensor_copy(
                        impflb[:], impfl[:]), inc=vs)
                    P.op("tensor", lambda t: t.matmul(
                        PB[5][:, :], ones_rb1[0:1, :], impfh[0:1, :],
                        start=True, stop=False),
                        waits=[(vs, ev_hl2), (vs, ev_one)])
                    ev_ib = P.op("tensor", lambda t: t.matmul(
                        PB[5][:, :], ones_rb1[0:1, :], impflb[0:1, :],
                        start=False, stop=True), inc=ts)
                    ev_ipm = None
                    for qq in range(4):
                        def f(t, qq=qq):
                            return t.matmul(PB[6][:, qq:qq + 1],
                                            impfh[0:1, 128 * qq:128 * (qq + 1)],
                                            one1b[0:1, 0:1],
                                            start=True, stop=False)
                        P.op("tensor", f)
                        def f2(t, qq=qq):
                            return t.matmul(PB[6][:, qq:qq + 1],
                                            impflb[0:1, 128 * qq:128 * (qq + 1)],
                                            one1b[0:1, 0:1],
                                            start=False, stop=True)
                        ev_ipm = P.op("tensor", f2,
                                      inc=ts if qq == 3 else None)
                    ev_ibc = P.op("scalar", lambda sc: sc.copy(impB[:], PB[5][:]),
                                  waits=[(ts, ev_ib)], inc=ss, sdep=None)
                    ev_ipc = P.op("scalar",
                                  lambda sc: sc.copy(impP[:], PB[6][:, 0:4]),
                                  waits=[(ts, ev_ipm)], inc=ss, sdep=None)
                    for qq in range(4):
                        P.op("vector", lambda v, qq=qq: v.tensor_scalar(
                            gt512[:], impB[:], impP[:, qq:qq + 1], None,
                            ALU.is_gt),
                            waits=[(ss, ev_ipc), (ss, ev_ibc)], sdep=None)
                        P.op("vector", lambda v, qq=qq: v.reduce_sum(
                            rank4[:, qq:qq + 1], gt512[:], AX))
                    P.op("vector", lambda v: v.tensor_scalar(
                        rank4[:], rank4[:], 255.5, None, ALU.is_lt))
                    P.op("vector", lambda v: v.tensor_scalar(
                        rank4[:], rank4[:], 1.0e30, -1.0e30, ALU.mult, ALU.add))
                    ev_mv = P.op("vector", lambda v: v.tensor_scalar(
                        rank4[:], rank4[:], -20.0, None, ALU.add), inc=vs)
                    ev_m20 = P.op("vector", lambda v: v.memset(m768[:], -20.0),
                                  inc=vs)
                    ev_md = P.dma("sync", qs, mask_d[:], m768[:],
                                  waits=[(vs, ev_m20)])
                    ev_md2 = P.dma(
                        "sync", qs,
                        mask_d[0:1, 1:513].rearrange("a (q p) -> (a p) q", p=128),
                        rank4[:], waits=[(vs, ev_mv), (qs, ev_md)],
                        allow_slow_non_contiguous=True)
                    ev_mb = None
                    for j in range(5):
                        ev_mb = P.dma("sync", qs, maskb[:, j:j + 1],
                                      mask_d[0:1, 128 * j:128 * (j + 1)]
                                      .rearrange("a p -> p a"),
                                      waits=[(qs, ev_md2)])
                    MASK_READY = (qs, ev_mb)

            LAST_V = (qs, ev_vout_dma)
            P.emit_phase(block)

        with ExitStack() as s2:
            pvT = sb(s2, "pvT", [128, HPC * S], BF16)
            qbuf = [sb(s2, f"qb{i}", [128, 512], BF16) for i in range(4)]
            kbuf = [sb(s2, f"kb{i}", [128, 2048], BF16) for i in range(4)]
            vbuf = [sb(s2, f"vb{i}", [128, 16 * 128], BF16) for i in range(4)]
            pT = [sb(s2, f"pt{i}", [128, 512], BF16) for i in range(4)]
            rcp1 = sb(s2, "rcp1", [1, 512], F32)
            den_sb = sb(s2, "den_sb", [1, 512], F32)
            lnb = sb(s2, "lnb", [1, 512], F32)
            rhi = sb(s2, "rhi", [1, 512], BF16)
            rlo = sb(s2, "rlo", [1, 512], BF16)
            rlf = sb(s2, "rlf", [1, 512], F32)
            ones_rb = sb(s2, "ones_rb", [1, 128], BF16)
            rep_sb = sb(s2, "rep_sb", [128, 512], F32)
            ones_bf = sb(s2, "ones_bf", [128, 1], BF16)
            dq_pk = [sb(s2, f"dqp{i}", [128, 1024], I16) for i in range(2)]
            dq_sc = sb(s2, "dqs", [128, 1024], F32)
            dq_mk = sb(s2, "dqm", [128, 1024], F32)
            dq_lo = sb(s2, "dql", [128, 1024], I16)
            dq_w = [sb(s2, f"dqw{i}", [128, 1024], BF16) for i in range(2)]
            owT = [sb(s2, f"ow{i}", [128, 16 * 512], BF16) for i in range(2)]
            obuf = [sb(s2, f"ob{i}", [128, 512], BF16) for i in range(4)]

            bar = P.barrier_waits()
            P.op("vector", lambda v: v.memset(ones_rb[:], 1.0), waits=bar)
            ev_vb = P.op("vector", lambda v: v.memset(ones_bf[:], 1.0),
                         inc=vs)

            dq_ev_v = {}
            dq_ev_st = {}

            def emit_dq_step(i):
                pt_i, ch = i // 4, i % 4
                cols = slice(1024 * ch, 1024 * (ch + 1))
                b = i % 2
                w_pk = list(bar) if i == 0 else []
                if i >= 2:
                    w_pk.append((vs, dq_ev_v[i - 2]))
                e_pk = P.dma("sync", qs, dq_pk[b][:],
                             pkT_in[128 * pt_i:128 * (pt_i + 1), cols],
                             waits=w_pk)
                w_sc = list(bar) if i == 0 else [(vs, dq_ev_v[i - 1])]
                e_sc = P.dma("gpsimd", qg, dq_sc[:],
                             scP_in[128 * pt_i:128 * (pt_i + 1), cols],
                             waits=w_sc)
                e_mk = P.dma("gpsimd", qg, dq_mk[:],
                             mkP_in[128 * pt_i:128 * (pt_i + 1), cols])
                e_me = P.op("gpsimd", lambda g: g.tensor_scalar(
                    dq_mk[:], dq_mk[:], 0.5, 0.5, ALU.mult, ALU.add),
                    waits=[(qg, e_mk)], inc=gs)
                e_se = P.op("gpsimd", lambda g: g.tensor_tensor(
                    dq_mk[:], dq_sc[:], dq_mk[:], ALU.mult),
                    waits=[(qg, e_sc), (gs, e_me)], inc=gs)
                w_w = [(qa, dq_ev_st[i - 1])] if i >= 1 else []
                e_an = P.op("vector", lambda v, b=b: v.tensor_scalar(
                    dq_lo[:], dq_pk[b][:], 15, None, ALU.bitwise_and),
                    waits=[(qs, e_pk)], inc=vs, sdep=None)
                e_sh = P.op("vector", lambda v, b=b: v.tensor_scalar(
                    dq_pk[b][:], dq_pk[b][:], 4, None,
                    ALU.logical_shift_right), inc=vs, sdep=None)
                P.op("vector", lambda v: v.scalar_tensor_tensor(
                    dq_w[0][:], dq_lo[:], -8.0, dq_mk[:], ALU.add, ALU.mult),
                    waits=w_w + [(gs, e_se)], sdep=e_an)
                ev_v = P.op("vector", lambda v, b=b: v.scalar_tensor_tensor(
                    dq_w[1][:], dq_pk[b][:], -8.0, dq_mk[:], ALU.add, ALU.mult),
                    inc=vs, sdep=e_sh)
                dq_ev_v[i] = ev_v
                P.dma("scalar", qa,
                      wdq_d[4 * ch:4 * (ch + 1), :, pt_i, :]
                      .transpose([1, 0, 2]),
                      dq_w[0].rearrange("p (o w) -> p o w", w=256),
                      waits=[(vs, ev_v)])
                dq_ev_st[i] = P.dma(
                    "scalar", qa,
                    wdq_d[4 * ch:4 * (ch + 1), :, 16 + pt_i, :]
                    .transpose([1, 0, 2]),
                    dq_w[1].rearrange("p (o w) -> p o w", w=256))

            units = [(h, c) for c in (3, 2, 1, 0) for h in range(HPC)]
            ev_qload = {}
            ev_kload = {}
            ev_vload = {}
            ev_unit_ts = {}
            g_exp = {}
            g_pvden = {}
            ev_evacv = {}
            ev_repev = {}
            ev_repev_s = {}
            ev_rcd = {}
            step = 0
            dq_i = 0

            def issue_loads(idx):
                h, c = units[idx]
                b = idx % 4
                J = 4 * c + 4
                w = []
                if idx >= 4:
                    w.append((ts, ev_unit_ts[idx - 4]))
                else:
                    w.extend(bar)
                    w.append(LAST_V)
                ev_qload[idx] = P.dma("sync", qs, qbuf[b][:],
                                      qT_d[h, :, 512 * c:512 * (c + 1)],
                                      waits=w)
                ev_kload[idx] = P.dma("sync", qs, kbuf[b][:, 0:128 * J],
                                      kT_d[h, :, 0:128 * J])
                vv = vbuf[b].rearrange("p (j w) -> p j w", w=128)
                ev_vload[idx] = P.dma(
                    "sync", qs, vv[:, 0:J, :],
                    v_d[0:128 * J, 128 * h:128 * (h + 1)].rearrange(
                        "(j p) w -> p j w", p=128))

            for _i in range(4):
                issue_loads(_i)
            pending_tail = None
            pend = None
            ev_rep_prev = None
            ev_hl_prev = None
            for idx, (h, c) in enumerate(units):
                b = idx % 4
                u2 = idx % 2
                J = 4 * c + 4
                korder = (list(range(5, J)) + [0, 1, 2, 3, 4])[:J] if J > 4 \
                    else list(range(J))
                vv = vbuf[b].rearrange("p (j w) -> p j w", w=128)
                nk = len(korder)

                def emit_pvden(sp, jn, j, ev_vx):
                    pw = [(ss, g_exp[sp]), (qs, ev_vload[idx])]
                    if ev_vx is not None:
                        pw.append((vs, ev_vx))
                    if jn == 0:
                        if idx >= 2:
                            pw.append((vs, ev_evacv[idx - 2]))
                        else:
                            pw.append((vs, ev_if))
                    P.op("tensor", lambda t, sp=sp, j=j, jn=jn, vv=vv,
                         u2=u2, nk=nk:
                         t.matmul(PB[2 + u2][:, :], vv[:, j, :],
                                  pT[sp % 4][:, :],
                                  start=(jn == 0), stop=(jn == nk - 1)),
                         waits=pw)
                    dw = []
                    if jn == 0:
                        dw.append((vs, ev_vb))
                        if idx >= 2:
                            dw.append((ss, ev_rcd[idx - 2]))
                        elif idx == 0:
                            dw.append((vs, ev_if))
                        else:
                            dw.append((ss, ev_ibc))
                    g_pvden[sp] = P.op(
                        "tensor", lambda t, sp=sp, jn=jn, u2=u2, nk=nk:
                        t.matmul(PB[4 + u2][0:1, :], ones_bf[:, 0:1],
                                 pT[sp % 4][:, :],
                                 start=(jn == 0), stop=(jn == nk - 1)),
                        waits=dw, inc=ts)

                for jn, j in enumerate(korder):
                    ab = [PB[0], PB[1], PB[6]][step % 3]
                    aw = [(qs, ev_kload[idx]), (qs, ev_qload[idx])]
                    if step >= 3:
                        aw.append((ss, g_exp[step - 3]))
                    ev_att = P.op(
                        "tensor", lambda t, ab=ab, j=j, b=b:
                        t.matmul(ab[:], kbuf[b][:, 128 * j:128 * (j + 1)],
                                 qbuf[b][:], start=True, stop=True),
                        waits=aw, inc=ts)
                    d = j - 4 * c
                    isdiag = 0 <= d <= 3
                    col0 = 128 * d if isdiag else 0
                    ew = [(ts, ev_att)]
                    if step >= 4:
                        ew.append((ts, g_pvden[step - 4]))
                    if j <= 4:
                        ew.append(MASK_READY)
                    bias = maskb[:, j:j + 1] if j <= 4 else b20[:, 0:1]
                    g_exp[step] = P.op(
                        "scalar", lambda sc, sp=step, ab=ab, col0=col0,
                        bias=bias:
                        sc.activation(pT[sp % 4][:, col0:512], ab[:, col0:512],
                                      AF.Exp, bias=bias, scale=SCALE),
                        waits=ew, inc=ss, sdep=None)
                    ev_vx = None
                    if isdiag and d > 0:
                        P.op("vector", lambda v, sp=step, col0=col0:
                             v.memset(pT[sp % 4][:, 0:col0], 0.0),
                             waits=[(ss, g_exp[step])], sdep=None)
                    if isdiag:
                        ev_vx = P.op(
                            "vector", lambda v, sp=step, d=d:
                            v.tensor_tensor(
                                pT[sp % 4][:, 128 * d:128 * (d + 1)],
                                pT[sp % 4][:, 128 * d:128 * (d + 1)],
                                tri[:], ALU.mult),
                            waits=[(ss, g_exp[step]), (gs, ev_tri)], inc=vs,
                            sdep=None)
                    if pend is not None:
                        emit_pvden(*pend)
                    pend = (step, jn, j, ev_vx)
                    step += 1
                    if jn == 1 and pending_tail is not None:
                        pending_tail()
                        pending_tail = None
                        if dq_i < 64:
                            emit_dq_step(dq_i)
                            dq_i += 1
                if pend is not None:
                    emit_pvden(*pend)
                    pend = None
                def make_tail(idx=idx, h=h, c=c, u2=u2, lastden=g_pvden[step - 1]):
                    def tail():
                        nonlocal ev_rep_prev, ev_hl_prev
                        dew = [(ts, lastden)]
                        if idx >= 1:
                            dew.append((vs, ev_hl_prev))
                        ev_de = P.op("scalar", lambda sc:
                                     sc.copy(den_sb[:], PB[4 + u2][0:1, :]),
                                     waits=dew, inc=ss, sdep=None)
                        ev_rcd[idx] = ev_de
                        P.op("scalar", lambda sc: sc.activation(
                            lnb[:], den_sb[:], AF.Ln), sdep=ev_de)
                        ev_rc = P.op("scalar", lambda sc: sc.activation(
                            rcp1[:], lnb[:], AF.Exp, scale=-1.0), inc=ss)
                        hw_ = [(ss, ev_rc)]
                        if idx >= 1:
                            hw_.append((ts, ev_rep_prev))
                        P.op("vector", lambda v: v.tensor_copy(rhi[:], rcp1[:]),
                             waits=hw_, sdep=None)
                        P.op("vector", lambda v: v.tensor_tensor(
                            rlf[:], rcp1[:], rhi[:], ALU.subtract))
                        ev_hl = P.op("vector",
                                     lambda v: v.tensor_copy(rlo[:], rlf[:]),
                                     inc=vs)
                        rw = [(vs, ev_hl)]
                        if idx >= 1:
                            rw.append((ss, ev_repev_s[idx - 1]))
                        else:
                            rw.append((ss, ev_ipc))
                        P.op("tensor", lambda t: t.matmul(
                            PBt.ap().bitcast(F32)[:, 0:512], ones_rb[0:1, :],
                            rhi[0:1, :], start=True, stop=False), waits=rw)
                        ev_rep = P.op("tensor", lambda t: t.matmul(
                            PBt.ap().bitcast(F32)[:, 0:512], ones_rb[0:1, :],
                            rlo[0:1, :], start=False, stop=True), inc=ts)
                        ev_rep_prev = ev_rep
                        ev_hl_prev = ev_hl
                        erw = [(ts, ev_rep)]
                        if idx >= 1:
                            erw.append((vs, ev_evacv[idx - 1]))
                        ev_repev_s[idx] = P.op(
                            "scalar", lambda sc: sc.copy(
                                rep_sb[:], PBt.ap().bitcast(F32)[:, 0:512]),
                            waits=erw, inc=ss, sdep=None)
                        ev_evacv[idx] = P.op(
                            "vector", lambda v:
                            v.tensor_tensor(
                                pvT[:, S * h + 512 * c:S * h + 512 * (c + 1)],
                                PB[2 + u2][:, :], rep_sb[:], ALU.mult),
                            waits=[(ss, ev_repev_s[idx]), (ts, lastden)],
                            inc=vs, sdep=None)
                    return tail
                pending_tail = make_tail()
                ev_unit_ts[idx] = g_pvden[step - 1]
                if idx + 4 < len(units):
                    issue_loads(idx + 4)

            if pending_tail is not None:
                pending_tail()
                pending_tail = None
            while dq_i < 64:
                emit_dq_step(dq_i)
                dq_i += 1
            last_exp = g_exp[step - 1]
            ATT_DONE = (vs, ev_evacv[len(units) - 1])
            LAST_EXPS = (ss, last_exp)

            ev_ow = {}
            obank_evac = {}
            ev_rsw = {}
            rs_done = {}
            oi = 0
            for half, cgrp in enumerate([(0, 2), (1, 3)]):
                for fc in range(8):
                    bo = oi % 2
                    w = []
                    if oi >= 2:
                        w.append((ts, ev_ow[oi - 2]))
                    else:
                        w.extend(bar)
                    e_ow = P.dma("gpsimd", qg,
                                 owT[bo].rearrange("p (k w) -> p k w", w=512),
                                 owT_in[:, 512 * fc:512 * (fc + 1)].rearrange(
                                     "(k p) w -> p k w", p=128), waits=w)
                    mm_ev = None
                    for ci in cgrp:
                        for sq_ in range(4):
                            bank = PB[sq_]
                            mw = [(qg, e_ow), ATT_DONE, LAST_EXPS]
                            pr = obank_evac.get(sq_)
                            if pr is not None:
                                mw.append((ss, pr))
                            for k in range(16):
                                def f(t, bank=bank, k=k, ci=ci, sq_=sq_, bo=bo):
                                    ow_ = owT[bo].rearrange(
                                        "p (kk w) -> p kk w", w=512)
                                    return t.matmul(
                                        bank[:],
                                        pvT[:, S * k + 512 * ci + 128 * sq_:
                                            S * k + 512 * ci + 128 * (sq_ + 1)],
                                        ow_[:, k, :],
                                        start=(k == 0), stop=(k == 15))
                                mm_ev = P.op("tensor", f,
                                             waits=mw if k == 0 else (),
                                             inc=ts if k == 15 else None)
                            ew2 = [(ts, mm_ev)]
                            if ("st", sq_) in ev_rsw:
                                ew2.append((qa, ev_rsw[("st", sq_)]))
                            eev = P.op("scalar",
                                       lambda sc, bank=bank, sq_=sq_:
                                       sc.copy(obuf[sq_][:], bank[:]),
                                       waits=ew2, inc=ss, sdep=None)
                            obank_evac[sq_] = eev
                            t0 = 512 * ci + 128 * sq_
                            r0 = (512 * ((t0 % 1024) // 256)
                                  + 256 * (t0 // 1024) + (t0 % 256))
                            ev_rsw[(ci, sq_, fc)] = P.dma(
                                "scalar", qa,
                                rsi_d[r0:r0 + 128, 512 * fc:512 * (fc + 1)],
                                obuf[sq_][:], waits=[(ss, eev)])
                            ev_rsw[("st", sq_)] = ev_rsw[(ci, sq_, fc)]
                    ev_ow[oi] = mm_ev
                    oi += 1
                for g in (2 * half, 2 * half + 1):
                    w_ = [(qa, ev_rsw[(cgrp[1], 3, 7)])]
                    rs_done[g] = P.op(
                        "gpsimd", lambda gp, g=g: gp.collective_compute(
                            "ReduceScatter", ALU.add, replica_groups=pairs,
                            ins=[rsi_d[512 * g:512 * (g + 1), :].opt()],
                            outs=[rso_d[256 * g:256 * (g + 1), :].opt()]),
                        waits=w_, inc=ccol)
            P.emit_phase(block)

        with ExitStack() as s3:
            a_sb = sb(s3, "a_sb", [128, 8 * DM], BF16)
            bias_sb = sb(s3, "bias_sb", [128, DM], F32)
            ssum = sb(s3, "ssum", [128, 4 * 16], F32)
            ssq = sb(s3, "ssq", [128, 4 * 16], F32)
            stat2 = sb(s3, "stat2", [128, 8], F32)

            bar2 = P.barrier_waits()
            ev_bias = P.dma("sync", qs, bias_sb[:], biasr_in[:], waits=bar2)

            ev_a = {}
            if True:
                osum = [sb(s3, f"osum{i}", [128, DM], BF16) for i in range(2)]
                xres = sb(s3, "xres_t", [128, DM], F32)
                r1 = sb(s3, "r1", [128, DM], F32)
                ev_r1 = {}
                ev_sqd = {}
                ev_nrm = {}
                for tt in range(8):
                    g = tt // 2
                    bb = tt % 2
                    w = [(ccol, rs_done[g])] + list(bar2)
                    if tt >= 2:
                        w.append((ss, ev_sqd[tt - 2]))
                    e_os = P.dma("sync", qs, osum[bb][:],
                                 rso_d[128 * tt:128 * (tt + 1), :], waits=w)
                    w2 = [(vs, ev_r1[tt - 1])] if tt >= 1 else []
                    e_xr = P.dma("sync", qs, xres[:],
                                 xres_in[128 * tt:128 * (tt + 1), :], waits=w2)
                    rw = [(qs, e_xr), (qs, e_os)]
                    if tt >= 1:
                        rw.append((vs, ev_nrm[tt - 1]))
                        rw.append((ss, ev_sqd[tt - 1]))
                    ev_r1[tt] = P.op("vector", lambda v, bb=bb:
                                     v.scalar_tensor_tensor(
                                         r1[:], osum[bb][:], 1.0,
                                         xres[:], ALU.mult, ALU.add,
                                         accum_out=stat[:, 8 * bb:8 * bb + 1]),
                                     waits=rw, inc=vs, sdep=None)
                    ev_sqd[tt] = P.op("scalar", lambda sc, bb=bb:
                                      sc.activation(
                                          osum[bb][:], r1[:], AF.Square,
                                          accum_out=stat[:, 8 * bb + 1:
                                                         8 * bb + 2]),
                                      waits=[(vs, ev_r1[tt])], inc=ss,
                                      sdep=None)
                    P.op("vector", lambda v, bb=bb: v.tensor_scalar(
                        stat[:, 8 * bb + 2:8 * bb + 3],
                        stat[:, 8 * bb:8 * bb + 1], 1.0 / DM, None, ALU.mult),
                        sdep=ev_r1[tt])
                    P.op("vector", lambda v, bb=bb: v.tensor_scalar(
                        stat[:, 8 * bb + 3:8 * bb + 4],
                        stat[:, 8 * bb + 1:8 * bb + 2], 1.0 / DM, None,
                        ALU.mult), waits=[(ss, ev_sqd[tt])], sdep=None)
                    P.op("vector", lambda v, bb=bb: v.tensor_tensor(
                        stat[:, 8 * bb + 4:8 * bb + 5],
                        stat[:, 8 * bb + 2:8 * bb + 3],
                        stat[:, 8 * bb + 2:8 * bb + 3], ALU.mult))
                    ev_var = P.op("vector", lambda v, bb=bb: v.tensor_tensor(
                        stat[:, 8 * bb + 5:8 * bb + 6],
                        stat[:, 8 * bb + 3:8 * bb + 4],
                        stat[:, 8 * bb + 4:8 * bb + 5], ALU.subtract), inc=vs)
                    P.op("scalar", lambda sc, bb=bb: sc.activation(
                        stat[:, 8 * bb + 6:8 * bb + 7],
                        stat[:, 8 * bb + 5:8 * bb + 6], AF.Ln,
                        bias=beps[:, 0:1]),
                        waits=[(vs, ev_var)], sdep=None)
                    ev_rstd = P.op("scalar", lambda sc, bb=bb: sc.activation(
                        stat[:, 8 * bb + 7:8 * bb + 8],
                        stat[:, 8 * bb + 6:8 * bb + 7], AF.Exp, scale=-0.5),
                        inc=ss)
                    ev_nrm[tt] = P.op(
                        "vector", lambda v, tt=tt, bb=bb: v.tensor_scalar(
                            a_sb[:, DM * tt:DM * (tt + 1)], r1[:],
                            stat[:, 8 * bb + 2:8 * bb + 3],
                            stat[:, 8 * bb + 7:8 * bb + 8],
                            ALU.subtract, ALU.mult),
                        waits=[(ss, ev_rstd)], inc=vs, sdep=None)
                    ev_a[tt] = ev_nrm[tt]
            P4_DONE = (vs, ev_a[7])
            if dbg:
                evd = P.dma("sync", qs, dbg1_d[:], a_sb[:], waits=[P4_DONE])
                P4_DONE = (qs, evd)

            if True:
                aT = sb(s3, "aT", [128, 32 * 512], BF16)
                wsl = [sb(s3, f"ws{i}", [128, 32 * 256], BF16)
                       for i in range(2)]
                tbuf = sb(s3, "tbuf", [128, 256], F32)
                j256 = sb(s3, "j256", [128, 256], F32)
                outb = [sb(s3, f"outb{i}", [128, DM // 4], F32) for i in range(2)]

                last_out = None
                last_out_ev = {}
                ev_wsl = {}
                ev_fev = {}
                ev_sqf = {}
                ev_trev = None
                wi = 0
                for grp in range(2):
                    for it in range(32):
                        ev_tr = None
                        for tl in range(4):
                            tt = 4 * grp + tl
                            w = [P4_DONE] if dbg else \
                                [(vs, ev_a[4 * grp + 3])]
                            if ev_trev is not None:
                                w.append((ss, ev_trev))
                            ev_tr = P.op(
                                "tensor", lambda t, it=it, tl=tl, tt=tt:
                                t.transpose(PBt[:, 128 * tl:128 * (tl + 1)],
                                            a_sb[:, DM * tt + 128 * it:
                                                 DM * tt + 128 * (it + 1)],
                                            ident[:]),
                                waits=w if tl == 0 else (),
                                inc=ts if tl == 3 else None)
                        ev_trev = P.op("scalar", lambda sc, it=it:
                                       sc.copy(aT[:, 512 * it:512 * (it + 1)],
                                               PBt[:, 0:512]),
                                       waits=[(ts, ev_tr)], inc=ss)
                    AT_RDY = (ss, ev_trev)
                    if dbg and grp == 0:
                        evd2 = P.dma("sync", qs, dbg2_d[:], aT[:],
                                     waits=[AT_RDY])
                        AT_RDY = (qs, evd2)
                    for oc in range(16):
                        bw = oc % 2
                        w = [AT_RDY, (qa, dq_ev_st[63])]
                        if wi >= 2:
                            w.append((ts, ev_wsl[wi - 2]))
                        e_w = P.dma("sync", qs,
                                    wsl[bw].rearrange("p (k w) -> p k w",
                                                      w=256),
                                    wdq_d[oc], waits=w)
                        mm_ev = None
                        for k in range(32):
                            for sq_ in range(4):
                                bank = PB[sq_]
                                mw = []
                                if k == 0:
                                    mw = [(qs, e_w), AT_RDY]
                                    pr = ev_fev.get((wi - 1, sq_))
                                    if pr is not None:
                                        mw.append((vs, pr))
                                def f(t, bank=bank, k=k, sq_=sq_, bw=bw):
                                    wv_ = wsl[bw].rearrange(
                                        "p (kk w) -> p kk w", w=256)
                                    return t.matmul(
                                        bank[:, 0:256],
                                        aT[:, 512 * k + 128 * sq_:
                                           512 * k + 128 * (sq_ + 1)],
                                        wv_[:, k, :],
                                        start=(k == 0), stop=(k == 31))
                                mm_ev = P.op(
                                    "tensor", f, waits=mw,
                                    inc=ts if (k == 31 and sq_ == 3) else None)
                        ev_wsl[wi] = mm_ev
                        for sq_ in range(4):
                            tt = 4 * grp + sq_
                            asl = a_sb[:, DM * tt + 256 * oc:
                                       DM * tt + 256 * (oc + 1)]
                            P.op("vector", lambda v, sq_=sq_, oc=oc:
                                 v.tensor_tensor(
                                     tbuf[:], PB[sq_][:, 0:256],
                                     bias_sb[:, 256 * oc:256 * (oc + 1)],
                                     ALU.add),
                                 waits=[(ts, mm_ev), (qs, ev_bias)],
                                 sdep=None)
                            ev_res = P.op(
                                "vector", lambda v, asl=asl, sq_=sq_, oc=oc:
                                v.scalar_tensor_tensor(
                                    asl, tbuf[:], 1.0, asl, ALU.mult,
                                    ALU.add,
                                    accum_out=ssum[:, 16 * sq_ + oc:
                                                   16 * sq_ + oc + 1]),
                                inc=vs)
                            ev_fev[(wi, sq_)] = ev_res
                            ev_sqf[(wi, sq_)] = P.op(
                                "scalar", lambda sc, asl=asl, sq_=sq_, oc=oc:
                                sc.activation(
                                    j256[:], asl, AF.Square,
                                    accum_out=ssq[:, 16 * sq_ + oc:
                                                  16 * sq_ + oc + 1]),
                                waits=[(vs, ev_res)], inc=ss, sdep=None)
                        wi += 1
                    for sq_ in range(4):
                        tt = 4 * grp + sq_
                        w = [(vs, ev_fev[(wi - 1, sq_)])]
                        P.op("vector", lambda v, sq_=sq_: v.reduce_sum(
                            stat2[:, 0:1], ssum[:, 16 * sq_:16 * (sq_ + 1)],
                            AX), waits=w)
                        P.op("vector", lambda v, sq_=sq_: v.reduce_sum(
                            stat2[:, 1:2], ssq[:, 16 * sq_:16 * (sq_ + 1)],
                            AX), waits=[(ss, ev_sqf[(wi - 1, sq_)])])
                        P.op("vector", lambda v: v.tensor_scalar(
                            stat2[:, 2:3], stat2[:, 0:1], 1.0 / DM, None,
                            ALU.mult))
                        P.op("vector", lambda v: v.tensor_scalar(
                            stat2[:, 3:4], stat2[:, 1:2], 1.0 / DM, None,
                            ALU.mult))
                        P.op("vector", lambda v: v.tensor_tensor(
                            stat2[:, 4:5], stat2[:, 2:3], stat2[:, 2:3],
                            ALU.mult))
                        ev_var = P.op("vector", lambda v: v.tensor_tensor(
                            stat2[:, 5:6], stat2[:, 3:4], stat2[:, 4:5],
                            ALU.subtract), inc=vs)
                        P.op("scalar", lambda sc: sc.activation(
                            stat2[:, 6:7], stat2[:, 5:6], AF.Ln, bias=beps[:, 0:1]),
                            waits=[(vs, ev_var)])
                        ev_rstd = P.op("scalar", lambda sc: sc.activation(
                            stat2[:, 7:8], stat2[:, 6:7], AF.Exp, scale=-0.5),
                            inc=ss)
                        for hf in range(4):
                            bo2 = (4 * tt + hf) % 2
                            ob = outb[bo2]
                            ow = [(ss, ev_rstd)]
                            if last_out_ev.get(bo2) is not None:
                                ow.append((qa, last_out_ev[bo2]))
                            ev_o = P.op(
                                "vector", lambda v, tt=tt, ob=ob, hf=hf:
                                v.tensor_scalar(
                                    ob[:],
                                    a_sb[:, DM * tt + 1024 * hf:
                                         DM * tt + 1024 * (hf + 1)],
                                    stat2[:, 2:3], stat2[:, 7:8],
                                    ALU.subtract, ALU.mult),
                                waits=ow, inc=vs, sdep=None)
                            last_out = P.dma(
                                "scalar", qa,
                                out_ext[128 * tt:128 * (tt + 1),
                                        1024 * hf:1024 * (hf + 1)],
                                ob[:], waits=[(vs, ev_o)])
                            last_out_ev[bo2] = last_out
                fin = qa.n
                fin_qs = qs.n
                P.op("scalar", lambda sc: sc.nop(),
                     waits=[(qa, fin), (qs, fin_qs)])
            P.emit_phase(block)

    return nc


def _rope_tables():
    inv_freq = 1.0 / (10000.0 ** (np.arange(0, D, 2, dtype=np.float64) / D))
    freqs = np.arange(S, dtype=np.float64)[:, None] * inv_freq[None, :]
    emb = np.concatenate([freqs, freqs], axis=-1)
    cos = np.ascontiguousarray(np.cos(emb).T.astype(np.float32))
    sin = np.sin(emb).T.astype(np.float32)
    sgn = np.ones((D, 1), np.float32)
    sgn[:64] = -1.0
    return cos, np.ascontiguousarray(sin * sgn)


def kernel(L0_attn_hidden_states, L0_attn_q_w, L0_attn_k_w, L0_attn_v_w,
           L0_attn_o_w, L0_ffn_weight_packed, L0_ffn_weight_scale,
           L0_ffn_weight_var_mask, L0_ffn_bias):
    x = np.asarray(L0_attn_hidden_states)
    qw = np.asarray(L0_attn_q_w)
    kw = np.asarray(L0_attn_k_w)
    vw = np.asarray(L0_attn_v_w)
    ow = np.asarray(L0_attn_o_w)
    pk = np.asarray(L0_ffn_weight_packed)
    sc = np.asarray(L0_ffn_weight_scale)
    mk = np.asarray(L0_ffn_weight_var_mask)
    bias = np.asarray(L0_ffn_bias)

    perm = np.concatenate([np.arange(0, DM, 2), np.arange(1, DM, 2)])
    cos, sin = _rope_tables()

    p16 = np.ascontiguousarray(pk.view(np.int16)[:, 0::2])
    p16 = p16.reshape(DM, 128, 16).transpose(1, 2, 0).reshape(2048, DM)
    pkT = np.ascontiguousarray(p16[:, perm])
    scP = np.ascontiguousarray(np.repeat(sc.reshape(DM, 128)[perm].T, 16,
                                         axis=0))
    mkP = np.ascontiguousarray(np.repeat(mk.reshape(DM, 128)[perm].T
                                         .view(np.uint8), 16, axis=0))
    biasr = np.ascontiguousarray(np.broadcast_to(bias[perm][None, :],
                                                 (128, DM)))

    in_maps = []
    for c in range(8):
        b, hh = c // 2, c % 2
        rows = slice(2048 * hh, 2048 * (hh + 1))
        in_maps.append({
            "xT": np.ascontiguousarray(x[b].T),
            "xres": np.ascontiguousarray(
                x[b, 1024 * hh:1024 * (hh + 1)][:, perm]),
            "qwT": np.ascontiguousarray(qw[rows].T),
            "kwT": np.ascontiguousarray(kw[rows].T),
            "vwT": np.ascontiguousarray(vw[rows].T),
            "owT": np.ascontiguousarray(ow[perm][:, rows].T),
            "pkT": pkT, "scP": scP, "mkP": mkP, "biasr": biasr,
            "cosD": cos, "sinD": sin,
            "flag": np.array([[1.0 if c < 2 else 0.0]], np.float32),
        })

    nc = build()
    res = run_bass_kernel_spmd(nc, in_maps, core_ids=list(range(8)))

    out = np.empty((4, S, DM), np.float32)
    inv = np.empty(DM, np.int64)
    inv[perm] = np.arange(DM)
    for c in range(8):
        b, hh = c // 2, c % 2
        out[b, 1024 * hh:1024 * (hh + 1)] = res.results[c]["out"][:, inv]
    return out


# version 17
# speedup vs baseline: 1.1240x; 1.1240x over previous
import numpy as np
import concourse.bass as bass
import concourse.mybir as mybir
from concourse.bass_utils import run_bass_kernel_spmd
from contextlib import ExitStack

F32 = mybir.dt.float32
BF16 = mybir.dt.bfloat16
F16 = mybir.dt.float16
I16 = mybir.dt.int16
U8 = mybir.dt.uint8
ALU = mybir.AluOpType
AF = mybir.ActivationFunctionType
AX = mybir.AxisListType.X

S = 2048
DM = 4096
D = 128
HPC = 16
SCALE = 1.0 / float(np.sqrt(128.0))
EXPB = -20.0
EPS = 1e-5
VIS_LO, VIS_HI = 1, 513
TXT_LO, TXT_HI = 513, 1992


class Sem:
    __slots__ = ("h", "n", "name")

    def __init__(self, h, name):
        self.h = h
        self.n = 0
        self.name = name


class Prog:
    ENGINES = ("tensor", "vector", "scalar", "gpsimd", "sync")

    def __init__(self, nc, stack):
        self.nc = nc
        self.stack = stack
        self.ops = {e: [] for e in self.ENGINES}
        self.last = {e: None for e in self.ENGINES}
        self.auto_serial = {}
        self._serial_prev = {}

    def sem(self, name):
        return Sem(self.stack.enter_context(self.nc.semaphore(name)), name)

    def op(self, eng, fn, waits=(), inc=None, inck=1, is_dma=False,
           sdep="auto"):
        waits = tuple((s, v) for (s, v) in waits if v is not None)
        ser = self.auto_serial.get(eng)
        if ser is not None:
            if sdep == "auto":
                sp = self._serial_prev.get(eng)
                if sp is not None:
                    waits = waits + (sp,)
            elif sdep is not None:
                waits = waits + ((ser, sdep),)
            if not is_dma and inc is None:
                inc = ser
        ev = None
        if inc is not None:
            inc.n += inck
            ev = inc.n
            self.last[eng] = (inc, ev)
            if ser is not None and not is_dma:
                self._serial_prev[eng] = (inc, ev)
        self.ops[eng].append((fn, waits, inc, inck))
        return ev

    def dma(self, q, qsem, out, in_, waits=(), **kw):
        def fn(e, out=out, in_=in_, kw=kw):
            return e.dma_start(out=out, in_=in_, **kw)
        return self.op(q, fn, waits, inc=qsem, inck=16, is_dma=True)

    def barrier_waits(self):
        return tuple(self.last[e] for e in self.ENGINES if self.last[e])

    def emit_phase(self, block):
        for eng in self.ENGINES:
            ops = self.ops[eng]
            if not ops:
                continue
            getattr(block, eng)(self._mk_body(ops))
        self.ops = {e: [] for e in self.ENGINES}

    def _mk_body(self, ops):
        def body(e):
            waited = {}
            for (fn, waits, inc, inck) in ops:
                for (s, v) in waits:
                    if waited.get(id(s), -1) >= v:
                        continue
                    e.wait_ge(s.h, v)
                    waited[id(s)] = v
                inst = fn(e)
                if inc is not None:
                    inst.then_inc(inc.h, inck)
        return body


def build(dbg=False):
    nc = bass.Bass("TRN2", target_bir_lowering=False,
                   detect_race_conditions=False)

    xT_in = nc.dram_tensor("xT", [DM, S], F32, kind="ExternalInput").ap()
    xres_in = nc.dram_tensor("xres", [1024, DM], F32, kind="ExternalInput").ap()
    qwT_in = nc.dram_tensor("qwT", [DM, 2048], F32, kind="ExternalInput").ap()
    kwT_in = nc.dram_tensor("kwT", [DM, 2048], F32, kind="ExternalInput").ap()
    vwT_in = nc.dram_tensor("vwT", [DM, 2048], F32, kind="ExternalInput").ap()
    owT_in = nc.dram_tensor("owT", [2048, DM], F32, kind="ExternalInput").ap()
    pkT_in = nc.dram_tensor("pkT", [2048, DM], I16, kind="ExternalInput").ap()
    scP_in = nc.dram_tensor("scP", [2048, DM], F16, kind="ExternalInput").ap()
    mkP_in = nc.dram_tensor("mkP", [2048, DM], U8, kind="ExternalInput").ap()
    biasr_in = nc.dram_tensor("biasr", [128, DM], F32, kind="ExternalInput").ap()
    cosD_in = nc.dram_tensor("cosD", [128, S], F32, kind="ExternalInput").ap()
    sinD_in = nc.dram_tensor("sinD", [128, S], F32, kind="ExternalInput").ap()
    flag_in = nc.dram_tensor("flag", [1, 1], F32, kind="ExternalInput").ap()
    out_ext = nc.dram_tensor("out", [1024, DM], F32, kind="ExternalOutput").ap()

    qT_d = nc.dram_tensor("qT_d", [HPC, 128, S], BF16).ap()
    kT_d = nc.dram_tensor("kT_d", [HPC, 128, S], BF16).ap()
    v_d = nc.dram_tensor("v_d", [S, 2048], BF16).ap()
    wdq_d = nc.dram_tensor("wdq_d", [16, 128, 32, 256], BF16).ap()
    rsi_d = nc.dram_tensor("rsi_d", [S, DM], BF16).ap()
    rso_d = nc.dram_tensor("rso_d", [1024, DM], BF16).ap()
    impi_d = nc.dram_tensor("impi_d", [1, 512], F32).ap()
    impo_d = nc.dram_tensor("impo_d", [1, 512], F32).ap()
    mask_d = nc.dram_tensor("mask_d", [1, 768], F32).ap()
    dbgkind = "ExternalOutput" if dbg == 2 else "Internal"
    dbg1_d = nc.dram_tensor("dbg1_d", [128, 8 * DM], BF16, kind=dbgkind).ap() \
        if dbg else None
    dbg2_d = nc.dram_tensor("dbg2_d", [128, 32 * 512], BF16, kind=dbgkind).ap() \
        if dbg else None
    dbg3_d = nc.dram_tensor("dbg3_d", [128, 8 * DM], BF16, kind=dbgkind).ap() \
        if dbg else None
    if dbg == 2:
        dbg_qT = nc.dram_tensor("dbg_qT", [HPC, 128, S], BF16, kind=dbgkind).ap()
        dbg_kT = nc.dram_tensor("dbg_kT", [HPC, 128, S], BF16, kind=dbgkind).ap()
        dbg_v = nc.dram_tensor("dbg_v", [S, 2048], BF16, kind=dbgkind).ap()
        dbg_w = nc.dram_tensor("dbg_w", [16, 128, 32, 256], BF16, kind=dbgkind).ap()
        dbg_rsi = nc.dram_tensor("dbg_rsi", [S, DM], BF16, kind=dbgkind).ap()
        dbg_rso = nc.dram_tensor("dbg_rso", [1024, DM], BF16, kind=dbgkind).ap()
        dbg_msk = nc.dram_tensor("dbg_msk", [1, 768], F32, kind=dbgkind).ap()

    pairs = [[0, 1], [2, 3], [4, 5], [6, 7]]
    allg = [[0, 1, 2, 3, 4, 5, 6, 7]]

    with ExitStack() as stack:
        P = Prog(nc, stack)
        qg = P.sem("qg")
        qs = P.sem("qs")
        qa = P.sem("qa")
        ccol = P.sem("ccol")
        ts = P.sem("ts")
        vs = P.sem("vs")
        ss = P.sem("ss")
        gs = P.sem("gs")
        P.auto_serial = {"vector": vs, "scalar": ss}

        PB = [stack.enter_context(nc.psum_tensor(f"pb{i}", [128, 512], F32))
              for i in range(7)]
        PBt = stack.enter_context(nc.psum_tensor("pbt", [128, 1024], BF16))

        sb = lambda st_, n, shp, dt: st_.enter_context(nc.sbuf_tensor(n, shp, dt))
        ident = sb(stack, "ident", [128, 128], BF16)
        tri = sb(stack, "tri", [128, 128], BF16)
        maskb = sb(stack, "maskb", [128, 8], F32)
        flag_sb = sb(stack, "flag_sb", [1, 1], F32)
        ones_r = sb(stack, "ones_r", [1, 128], F32)
        one1 = sb(stack, "one1", [1, 1], F32)
        stat = sb(stack, "stat", [128, 16], F32)
        b20 = sb(stack, "b20", [128, 1], F32)
        beps = sb(stack, "beps", [128, 1], F32)

        block = stack.enter_context(nc.Block())

        ev_flag = P.dma("sync", qs, flag_sb[:], flag_in[:])
        e1 = P.op("gpsimd", lambda g: g.memset(ident[:], 0.0), inc=gs)
        e2 = P.op("gpsimd", lambda g: g.affine_select(
            out=ident[:], in_=ident[:], compare_op=ALU.not_equal, fill=1.0,
            base=0, pattern=[[-1, 128]], channel_multiplier=1),
            waits=[(gs, e1)], inc=gs)
        e3 = P.op("gpsimd", lambda g: g.memset(tri[:], 1.0),
                  waits=[(gs, e2)], inc=gs)
        ev_tri = P.op("gpsimd", lambda g: g.affine_select(
            out=tri[:], in_=tri[:], compare_op=ALU.is_ge, fill=0.0,
            base=0, pattern=[[1, 128]], channel_multiplier=-1),
            waits=[(gs, e3)], inc=gs)
        P.op("vector", lambda v: v.memset(ones_r[:], 1.0))
        P.op("vector", lambda v: v.memset(b20[:], EXPB))
        P.op("vector", lambda v: v.memset(beps[:], EPS))
        ev_one = P.op("vector", lambda v: v.memset(one1[:], 1.0), inc=vs)
        P.emit_phase(block)

        MASK_READY = None
        with ExitStack() as s1:
            xT = sb(s1, "xT_sb", [128, 32 * S], BF16)
            wb = [sb(s1, f"wb{i}", [128, 32 * 256], BF16) for i in range(2)]
            cos_sb = sb(s1, "cos_sb", [128, 512], F32)
            sin_sb = sb(s1, "sin_sb", [128, 512], F32)
            kraw = sb(s1, "kraw", [128, 1024], F32)
            kp = sb(s1, "kp", [128, 1024], F32)
            robuf = sb(s1, "robuf", [128, 1024], BF16)
            redt = sb(s1, "redt", [128, 1], F32)
            vout = sb(s1, "vout", [128, 256], BF16)
            sq16 = sb(s1, "sq16", [128, 16], F32)
            sq16b = sb(s1, "sq16b", [128, 16], BF16)
            kvisb = sb(s1, "kvisb", [128, 16 * 512], BF16)
            impf = sb(s1, "impf", [1, 512], F32)
            impfh = sb(s1, "impfh", [1, 512], BF16)
            impfl = sb(s1, "impfl", [1, 512], F32)
            impflb = sb(s1, "impflb", [1, 512], BF16)
            ones_rb1 = sb(s1, "ones_rb1", [1, 128], BF16)
            one1b = sb(s1, "one1b", [1, 1], BF16)
            impB = sb(s1, "impB", [128, 512], F32)
            impP = sb(s1, "impP", [128, 4], F32)
            gt512 = sb(s1, "gt512", [128, 512], F32)
            rank4 = sb(s1, "rank4", [128, 4], F32)
            m768 = sb(s1, "m768", [1, 768], F32)

            xTv = xT.rearrange("p (k t) -> p k t", t=S)
            wbv = [w.rearrange("p (k w) -> p k w", w=256) for w in wb]

            P.op("vector", lambda v: v.memset(ones_rb1[:], 1.0))
            P.op("vector", lambda v: v.memset(one1b[:], 1.0))
            ev_sq0 = P.op("vector", lambda v: v.memset(sq16[:], 0.0), inc=vs)

            ev_xt = []
            ev_xt.append(P.dma(
                "gpsimd", qg, xTv[:, :, 0:512],
                xT_in[:, 0:512].rearrange("(k p) t -> p k t", p=128)))

            ev_wload = {}
            proj_evac = []
            ev_t2 = None
            ev_addk = None
            ev_robf = None
            ev_shift = None
            ev_sq = ev_sq0
            last_qs_store = None
            n_unit = 0
            v_unit = 0
            ev_vout_dma = None
            ev_vout_evac = []
            gi = 0

            for pi, wsrc in enumerate([qwT_in, kwT_in, vwT_in]):
                dst = [qT_d, kT_d, None][pi]
                for g in range(8):
                    waits = []
                    if gi >= 2:
                        waits.append((ts, ev_wload[gi - 2]))
                    ev_w = P.dma("gpsimd", qg, wbv[gi % 2][:, :, :],
                                 wsrc[:, 256 * g:256 * (g + 1)].rearrange(
                                     "(k p) w -> p k w", p=128), waits=waits)
                    if gi == 0:
                        for c_ in range(1, 4):
                            ev_xt.append(P.dma(
                                "gpsimd", qg,
                                xTv[:, :, 512 * c_:512 * (c_ + 1)],
                                xT_in[:, 512 * c_:512 * (c_ + 1)].rearrange(
                                    "(k p) t -> p k t", p=128)))
                    mm_ev = None
                    if pi < 2:
                        for c in range(4):
                            cw = [(qg, ev_w), (qg, ev_xt[c])]
                            if ev_t2 is not None:
                                cw.append((vs, ev_t2))
                            evc = P.dma("sync", qs, cos_sb[:],
                                        cosD_in[:, 512 * c:512 * (c + 1)],
                                        waits=cw)
                            evc2 = P.dma("sync", qs, sin_sb[:],
                                         sinD_in[:, 512 * c:512 * (c + 1)])
                            for hh in range(2):
                                bank = PB[n_unit % 2]
                                mw = [(qg, ev_w), (qg, ev_xt[c])]
                                if n_unit >= 2:
                                    mw.append((ss, proj_evac[n_unit - 2]))
                                for k in range(32):
                                    def f(t, bank=bank, k=k, hh=hh, c=c,
                                          wv=wbv[gi % 2]):
                                        return t.matmul(
                                            bank[:],
                                            wv[:, k, 128 * hh:128 * (hh + 1)],
                                            xTv[:, k, 512 * c:512 * (c + 1)],
                                            start=(k == 0), stop=(k == 31))
                                    mm_ev = P.op("tensor", f,
                                                 waits=mw if k == 0 else (),
                                                 inc=ts if k == 31 else None)
                                ew = [(ts, mm_ev)]
                                if ev_addk is not None:
                                    ew.append((vs, ev_addk))
                                if ev_shift is not None:
                                    ew.append((qs, ev_shift))
                                eev = P.op(
                                    "scalar", lambda sc, bank=bank, hh=hh:
                                    sc.copy(kraw[:, 512 * hh:512 * (hh + 1)],
                                            bank[:]), waits=ew, inc=ss,
                                    sdep=None)
                                proj_evac.append(eev)
                                n_unit += 1
                            shw = [(ss, proj_evac[-1])]
                            if ev_robf is not None:
                                shw.append((vs, ev_robf))
                            P.dma("sync", qs, kp[0:64, :], kraw[64:128, :],
                                  waits=shw)
                            ev_shift = P.dma("sync", qs, kp[64:128, :],
                                             kraw[0:64, :])
                            k3 = kraw.rearrange("p (h t) -> p h t", t=512)
                            p3 = kp.rearrange("p (h t) -> p h t", t=512)
                            cb = cos_sb[:].unsqueeze(1).broadcast_to([128, 2, 512])
                            sbx = sin_sb[:].unsqueeze(1).broadcast_to([128, 2, 512])
                            P.op("vector", lambda v, k3=k3, cb=cb:
                                 v.tensor_tensor(k3, k3, cb, ALU.mult),
                                 waits=[(qs, evc2), (qs, ev_shift)])
                            ev_t2 = P.op("vector", lambda v, p3=p3, sbx=sbx:
                                         v.tensor_tensor(p3, p3, sbx, ALU.mult),
                                         inc=vs)
                            rw = []
                            if last_qs_store is not None:
                                rw.append((qs, last_qs_store))
                            r3 = robuf.rearrange("p (h t) -> p h t", t=512)
                            ev_addk = P.op("vector", lambda v, r3=r3, k3=k3, p3=p3:
                                           v.tensor_tensor(r3, k3, p3, ALU.add),
                                           waits=rw, inc=vs)
                            ev_robf = ev_addk
                            if pi == 0:
                                tlo = max(TXT_LO, 512 * c) - 512 * c
                                thi = min(TXT_HI, 512 * (c + 1)) - 512 * c
                                if thi > tlo:
                                    for hh in range(2):
                                        hcol = 2 * g + hh
                                        P.op("vector",
                                             lambda v, hh=hh, tlo=tlo, thi=thi:
                                             v.reduce_sum(
                                                 redt[:],
                                                 robuf[:, 512 * hh + tlo:
                                                       512 * hh + thi], AX))
                                        ev_sq = P.op(
                                            "vector", lambda v, hcol=hcol:
                                            v.tensor_tensor(
                                                sq16[:, hcol:hcol + 1],
                                                sq16[:, hcol:hcol + 1],
                                                redt[:], ALU.add), inc=vs)
                                    ev_robf = ev_sq
                            for hh in range(2):
                                h = 2 * g + hh
                                last_qs_store = P.dma(
                                    "sync", qs,
                                    dst[h, :, 512 * c:512 * (c + 1)],
                                    robuf[:, 512 * hh:512 * (hh + 1)],
                                    waits=[(vs, ev_addk)])
                                if pi == 1 and c == 1:
                                    ev_kvl = P.dma(
                                        "sync", qs,
                                        kvisb[:, 512 * h:512 * (h + 1)],
                                        kT_d[h, :, VIS_LO:VIS_HI])
                    else:
                        for c in range(4):
                            for sub in range(4):
                                tt = 4 * c + sub
                                bank = PB[v_unit % 2]
                                mw = [(qg, ev_w), (qg, ev_xt[c])]
                                if v_unit >= 2:
                                    mw.append((ss, ev_vout_evac[v_unit - 2]))
                                for k in range(32):
                                    def f(t, bank=bank, k=k, tt=tt,
                                          wv=wbv[gi % 2]):
                                        return t.matmul(
                                            bank[:, 0:256],
                                            xTv[:, k, 128 * tt:128 * (tt + 1)],
                                            wv[:, k, :],
                                            start=(k == 0), stop=(k == 31))
                                    mm_ev = P.op("tensor", f,
                                                 waits=mw if k == 0 else (),
                                                 inc=ts if k == 31 else None)
                                ew = [(ts, mm_ev)]
                                if ev_vout_dma is not None:
                                    ew.append((qs, ev_vout_dma))
                                eev = P.op("scalar", lambda sc, bank=bank:
                                           sc.copy(vout[:], bank[:, 0:256]),
                                           waits=ew, inc=ss, sdep=None)
                                ev_vout_evac.append(eev)
                                ev_vout_dma = P.dma(
                                    "sync", qs,
                                    v_d[128 * tt:128 * (tt + 1),
                                        256 * g:256 * (g + 1)],
                                    vout[:], waits=[(ss, eev)])
                                v_unit += 1
                    ev_wload[gi] = mm_ev
                    gi += 1

                if pi == 1:
                    ev_k_st = last_qs_store
                    ev_sqb = P.op("vector",
                                  lambda v: v.tensor_copy(sq16b[:], sq16[:]),
                                  waits=[(vs, ev_sq)], inc=vs)
                    impev = None
                    for h in range(HPC):
                        def f(t, h=h):
                            return t.matmul(
                                PB[4][0:1, :], sq16b[:, h:h + 1],
                                kvisb[:, 512 * h:512 * (h + 1)],
                                start=(h == 0), stop=(h == HPC - 1))
                        w2 = [(qs, ev_kvl), (vs, ev_sqb)] if h == 0 else []
                        impev = P.op("tensor", f, waits=w2,
                                     inc=ts if h == HPC - 1 else None)
                    ev_if = P.op("vector", lambda v: v.tensor_scalar(
                        impf[:], PB[4][0:1, :], flag_sb[0:1, 0:1], None,
                        ALU.mult), waits=[(ts, impev), (qs, ev_flag)], inc=vs,
                        sdep=None)
                    ev_iw = P.dma("sync", qs, impi_d[:], impf[:],
                                  waits=[(vs, ev_if)])
                    ev_ar = P.op("gpsimd", lambda g: g.collective_compute(
                        "AllReduce", ALU.add, replica_groups=allg,
                        ins=[impi_d.opt()], outs=[impo_d.opt()]),
                        waits=[(qs, ev_iw)], inc=ccol)
                    ev_ir = P.dma("gpsimd", qg, impf[:], impo_d[:],
                                  waits=[(ccol, ev_ar)])
                    saved_ev_ir = ev_ir
                    saved_ev_if = ev_if

            if True:
                    ev_ir = saved_ev_ir
                    ev_if = saved_ev_if
                    ev_hi = P.op("vector", lambda v: v.tensor_copy(
                        impfh[:], impf[:]),
                        waits=[(qg, ev_ir), (vs, ev_if)], sdep=None)
                    P.op("vector", lambda v: v.tensor_tensor(
                        impfl[:], impf[:], impfh[:], ALU.subtract))
                    ev_hl2 = P.op("vector", lambda v: v.tensor_copy(
                        impflb[:], impfl[:]), inc=vs)
                    P.op("tensor", lambda t: t.matmul(
                        PB[5][:, :], ones_rb1[0:1, :], impfh[0:1, :],
                        start=True, stop=False),
                        waits=[(vs, ev_hl2), (vs, ev_one)])
                    ev_ib = P.op("tensor", lambda t: t.matmul(
                        PB[5][:, :], ones_rb1[0:1, :], impflb[0:1, :],
                        start=False, stop=True), inc=ts)
                    ev_ipm = None
                    for qq in range(4):
                        def f(t, qq=qq):
                            return t.matmul(PB[6][:, qq:qq + 1],
                                            impfh[0:1, 128 * qq:128 * (qq + 1)],
                                            one1b[0:1, 0:1],
                                            start=True, stop=False)
                        P.op("tensor", f)
                        def f2(t, qq=qq):
                            return t.matmul(PB[6][:, qq:qq + 1],
                                            impflb[0:1, 128 * qq:128 * (qq + 1)],
                                            one1b[0:1, 0:1],
                                            start=False, stop=True)
                        ev_ipm = P.op("tensor", f2,
                                      inc=ts if qq == 3 else None)
                    ev_ibc = P.op("scalar", lambda sc: sc.copy(impB[:], PB[5][:]),
                                  waits=[(ts, ev_ib)], inc=ss, sdep=None)
                    ev_ipc = P.op("scalar",
                                  lambda sc: sc.copy(impP[:], PB[6][:, 0:4]),
                                  waits=[(ts, ev_ipm)], inc=ss, sdep=None)
                    for qq in range(4):
                        P.op("vector", lambda v, qq=qq: v.tensor_scalar(
                            gt512[:], impB[:], impP[:, qq:qq + 1], None,
                            ALU.is_gt),
                            waits=[(ss, ev_ipc), (ss, ev_ibc)], sdep=None)
                        P.op("vector", lambda v, qq=qq: v.reduce_sum(
                            rank4[:, qq:qq + 1], gt512[:], AX))
                    P.op("vector", lambda v: v.tensor_scalar(
                        rank4[:], rank4[:], 255.5, None, ALU.is_lt))
                    P.op("vector", lambda v: v.tensor_scalar(
                        rank4[:], rank4[:], 1.0e30, -1.0e30, ALU.mult, ALU.add))
                    ev_mv = P.op("vector", lambda v: v.tensor_scalar(
                        rank4[:], rank4[:], -20.0, None, ALU.add), inc=vs)
                    ev_m20 = P.op("vector", lambda v: v.memset(m768[:], -20.0),
                                  inc=vs)
                    ev_md = P.dma("sync", qs, mask_d[:], m768[:],
                                  waits=[(vs, ev_m20)])
                    ev_md2 = P.dma(
                        "sync", qs,
                        mask_d[0:1, 1:513].rearrange("a (q p) -> (a p) q", p=128),
                        rank4[:], waits=[(vs, ev_mv), (qs, ev_md)],
                        allow_slow_non_contiguous=True)
                    ev_mb = None
                    for j in range(5):
                        ev_mb = P.dma("sync", qs, maskb[:, j:j + 1],
                                      mask_d[0:1, 128 * j:128 * (j + 1)]
                                      .rearrange("a p -> p a"),
                                      waits=[(qs, ev_md2)])
                    MASK_READY = (qs, ev_mb)

            LAST_V = (qs, ev_vout_dma)
            P.emit_phase(block)

        with ExitStack() as s2:
            pvT = sb(s2, "pvT", [128, HPC * S], BF16)
            qbuf = [sb(s2, f"qb{i}", [128, 512], BF16) for i in range(4)]
            kbuf = [sb(s2, f"kb{i}", [128, 2048], BF16) for i in range(4)]
            vbuf = [sb(s2, f"vb{i}", [128, 16 * 128], BF16) for i in range(4)]
            pT = [sb(s2, f"pt{i}", [128, 512], BF16) for i in range(4)]
            rcp1 = sb(s2, "rcp1", [1, 512], F32)
            den_sb = sb(s2, "den_sb", [1, 512], F32)
            lnb = sb(s2, "lnb", [1, 512], F32)
            rhi = sb(s2, "rhi", [1, 512], BF16)
            rlo = sb(s2, "rlo", [1, 512], BF16)
            rlf = sb(s2, "rlf", [1, 512], F32)
            ones_rb = sb(s2, "ones_rb", [1, 128], BF16)
            rep_sb = sb(s2, "rep_sb", [128, 512], F32)
            ones_bf = sb(s2, "ones_bf", [128, 1], BF16)
            dq_pk = [sb(s2, f"dqp{i}", [128, 1024], I16) for i in range(2)]
            dq_sc = sb(s2, "dqs", [128, 1024], F32)
            dq_mk = sb(s2, "dqm", [128, 1024], F32)
            dq_lo = sb(s2, "dql", [128, 1024], I16)
            dq_w = [sb(s2, f"dqw{i}", [128, 1024], BF16) for i in range(2)]
            owT = [sb(s2, f"ow{i}", [128, 16 * 512], BF16) for i in range(2)]
            obuf = [sb(s2, f"ob{i}", [128, 512], BF16) for i in range(4)]

            bar = P.barrier_waits()
            P.op("vector", lambda v: v.memset(ones_rb[:], 1.0), waits=bar)
            ev_vb = P.op("vector", lambda v: v.memset(ones_bf[:], 1.0),
                         inc=vs)

            dq_ev_v = {}
            dq_ev_st = {}

            def emit_dq_step(i):
                pt_i, ch = i // 4, i % 4
                cols = slice(1024 * ch, 1024 * (ch + 1))
                b = i % 2
                w_pk = list(bar) if i == 0 else []
                if i >= 2:
                    w_pk.append((vs, dq_ev_v[i - 2]))
                e_pk = P.dma("sync", qs, dq_pk[b][:],
                             pkT_in[128 * pt_i:128 * (pt_i + 1), cols],
                             waits=w_pk)
                w_sc = list(bar) if i == 0 else [(vs, dq_ev_v[i - 1])]
                e_sc = P.dma("gpsimd", qg, dq_sc[:],
                             scP_in[128 * pt_i:128 * (pt_i + 1), cols],
                             waits=w_sc)
                e_mk = P.dma("gpsimd", qg, dq_mk[:],
                             mkP_in[128 * pt_i:128 * (pt_i + 1), cols])
                e_me = P.op("gpsimd", lambda g: g.tensor_scalar(
                    dq_mk[:], dq_mk[:], 0.5, 0.5, ALU.mult, ALU.add),
                    waits=[(qg, e_mk)], inc=gs)
                e_se = P.op("gpsimd", lambda g: g.tensor_tensor(
                    dq_mk[:], dq_sc[:], dq_mk[:], ALU.mult),
                    waits=[(qg, e_sc), (gs, e_me)], inc=gs)
                w_w = [(qa, dq_ev_st[i - 1])] if i >= 1 else []
                e_an = P.op("vector", lambda v, b=b: v.tensor_scalar(
                    dq_lo[:], dq_pk[b][:], 15, None, ALU.bitwise_and),
                    waits=[(qs, e_pk)], inc=vs, sdep=None)
                e_sh = P.op("vector", lambda v, b=b: v.tensor_scalar(
                    dq_pk[b][:], dq_pk[b][:], 4, None,
                    ALU.logical_shift_right), inc=vs, sdep=None)
                P.op("vector", lambda v: v.scalar_tensor_tensor(
                    dq_w[0][:], dq_lo[:], -8.0, dq_mk[:], ALU.add, ALU.mult),
                    waits=w_w + [(gs, e_se)], sdep=e_an)
                ev_v = P.op("vector", lambda v, b=b: v.scalar_tensor_tensor(
                    dq_w[1][:], dq_pk[b][:], -8.0, dq_mk[:], ALU.add, ALU.mult),
                    inc=vs, sdep=e_sh)
                dq_ev_v[i] = ev_v
                P.dma("scalar", qa,
                      wdq_d[4 * ch:4 * (ch + 1), :, pt_i, :]
                      .transpose([1, 0, 2]),
                      dq_w[0].rearrange("p (o w) -> p o w", w=256),
                      waits=[(vs, ev_v)])
                dq_ev_st[i] = P.dma(
                    "scalar", qa,
                    wdq_d[4 * ch:4 * (ch + 1), :, 16 + pt_i, :]
                    .transpose([1, 0, 2]),
                    dq_w[1].rearrange("p (o w) -> p o w", w=256))

            units = [(h, c) for c in (3, 2, 1, 0) for h in range(HPC)]
            ev_qload = {}
            ev_kload = {}
            ev_vload = {}
            ev_unit_ts = {}
            g_exp = {}
            g_pvden = {}
            ev_evacv = {}
            ev_repev = {}
            ev_repev_s = {}
            ev_rcd = {}
            step = 0
            dq_i = 0

            def issue_loads(idx):
                h, c = units[idx]
                b = idx % 4
                J = 4 * c + 4
                w = []
                if idx >= 4:
                    w.append((ts, ev_unit_ts[idx - 4]))
                else:
                    w.extend(bar)
                    w.append(LAST_V)
                ev_qload[idx] = P.dma("sync", qs, qbuf[b][:],
                                      qT_d[h, :, 512 * c:512 * (c + 1)],
                                      waits=w)
                ev_kload[idx] = P.dma("sync", qs, kbuf[b][:, 0:128 * J],
                                      kT_d[h, :, 0:128 * J])
                vv = vbuf[b].rearrange("p (j w) -> p j w", w=128)
                ev_vload[idx] = P.dma(
                    "sync", qs, vv[:, 0:J, :],
                    v_d[0:128 * J, 128 * h:128 * (h + 1)].rearrange(
                        "(j p) w -> p j w", p=128))

            for _i in range(4):
                issue_loads(_i)
            pending_tail = None
            pend = []
            ev_rep_prev = None
            ev_hl_prev = None
            for idx, (h, c) in enumerate(units):
                b = idx % 4
                u2 = idx % 2
                J = 4 * c + 4
                korder = (list(range(5, J)) + [0, 1, 2, 3, 4])[:J] if J > 4 \
                    else list(range(J))
                vv = vbuf[b].rearrange("p (j w) -> p j w", w=128)
                nk = len(korder)

                def emit_pvden(sp, jn, j, ev_vx):
                    pw = [(ss, g_exp[sp]), (qs, ev_vload[idx])]
                    if ev_vx is not None:
                        pw.append((vs, ev_vx))
                    if jn == 0:
                        if idx >= 2:
                            pw.append((vs, ev_evacv[idx - 2]))
                        else:
                            pw.append((vs, ev_if))
                    P.op("tensor", lambda t, sp=sp, j=j, jn=jn, vv=vv,
                         u2=u2, nk=nk:
                         t.matmul(PB[2 + u2][:, :], vv[:, j, :],
                                  pT[sp % 4][:, :],
                                  start=(jn == 0), stop=(jn == nk - 1)),
                         waits=pw)
                    dw = []
                    if jn == 0:
                        dw.append((vs, ev_vb))
                        if idx >= 2:
                            dw.append((ss, ev_rcd[idx - 2]))
                        elif idx == 0:
                            dw.append((vs, ev_if))
                        else:
                            dw.append((ss, ev_ibc))
                    g_pvden[sp] = P.op(
                        "tensor", lambda t, sp=sp, jn=jn, u2=u2, nk=nk:
                        t.matmul(PB[4 + u2][0:1, :], ones_bf[:, 0:1],
                                 pT[sp % 4][:, :],
                                 start=(jn == 0), stop=(jn == nk - 1)),
                        waits=dw, inc=ts)

                for jn, j in enumerate(korder):
                    ab = [PB[0], PB[1], PB[6]][step % 3]
                    aw = [(qs, ev_kload[idx]), (qs, ev_qload[idx])]
                    if step >= 3:
                        aw.append((ss, g_exp[step - 3]))
                    ev_att = P.op(
                        "tensor", lambda t, ab=ab, j=j, b=b:
                        t.matmul(ab[:], kbuf[b][:, 128 * j:128 * (j + 1)],
                                 qbuf[b][:], start=True, stop=True),
                        waits=aw, inc=ts)
                    d = j - 4 * c
                    isdiag = 0 <= d <= 3
                    col0 = 128 * d if isdiag else 0
                    ew = [(ts, ev_att)]
                    if step >= 4:
                        ew.append((ts, g_pvden[step - 4]))
                    if j <= 4:
                        ew.append(MASK_READY)
                    bias = maskb[:, j:j + 1] if j <= 4 else b20[:, 0:1]
                    g_exp[step] = P.op(
                        "scalar", lambda sc, sp=step, ab=ab, col0=col0,
                        bias=bias:
                        sc.activation(pT[sp % 4][:, col0:512], ab[:, col0:512],
                                      AF.Exp, bias=bias, scale=SCALE),
                        waits=ew, inc=ss, sdep=None)
                    ev_vx = None
                    if isdiag and d > 0:
                        P.op("vector", lambda v, sp=step, col0=col0:
                             v.memset(pT[sp % 4][:, 0:col0], 0.0),
                             waits=[(ss, g_exp[step])], sdep=None)
                    if isdiag:
                        ev_vx = P.op(
                            "vector", lambda v, sp=step, d=d:
                            v.tensor_tensor(
                                pT[sp % 4][:, 128 * d:128 * (d + 1)],
                                pT[sp % 4][:, 128 * d:128 * (d + 1)],
                                tri[:], ALU.mult),
                            waits=[(ss, g_exp[step]), (gs, ev_tri)], inc=vs,
                            sdep=None)
                    if len(pend) == 2:
                        emit_pvden(*pend.pop(0))
                    pend.append((step, jn, j, ev_vx))
                    step += 1
                    if jn == 3 and pending_tail is not None:
                        pending_tail()
                        pending_tail = None
                        if dq_i < 64:
                            emit_dq_step(dq_i)
                            dq_i += 1
                while pend:
                    emit_pvden(*pend.pop(0))
                def make_tail(idx=idx, h=h, c=c, u2=u2, lastden=g_pvden[step - 1]):
                    def tail():
                        nonlocal ev_rep_prev, ev_hl_prev
                        dew = [(ts, lastden)]
                        if idx >= 1:
                            dew.append((vs, ev_hl_prev))
                        ev_de = P.op("scalar", lambda sc:
                                     sc.copy(den_sb[:], PB[4 + u2][0:1, :]),
                                     waits=dew, inc=ss, sdep=None)
                        ev_rcd[idx] = ev_de
                        P.op("scalar", lambda sc: sc.activation(
                            lnb[:], den_sb[:], AF.Ln), sdep=ev_de)
                        ev_rc = P.op("scalar", lambda sc: sc.activation(
                            rcp1[:], lnb[:], AF.Exp, scale=-1.0), inc=ss)
                        hw_ = [(ss, ev_rc)]
                        if idx >= 1:
                            hw_.append((ts, ev_rep_prev))
                        P.op("vector", lambda v: v.tensor_copy(rhi[:], rcp1[:]),
                             waits=hw_, sdep=None)
                        P.op("vector", lambda v: v.tensor_tensor(
                            rlf[:], rcp1[:], rhi[:], ALU.subtract))
                        ev_hl = P.op("vector",
                                     lambda v: v.tensor_copy(rlo[:], rlf[:]),
                                     inc=vs)
                        rw = [(vs, ev_hl)]
                        if idx >= 1:
                            rw.append((ss, ev_repev_s[idx - 1]))
                        else:
                            rw.append((ss, ev_ipc))
                        P.op("tensor", lambda t: t.matmul(
                            PBt.ap().bitcast(F32)[:, 0:512], ones_rb[0:1, :],
                            rhi[0:1, :], start=True, stop=False), waits=rw)
                        ev_rep = P.op("tensor", lambda t: t.matmul(
                            PBt.ap().bitcast(F32)[:, 0:512], ones_rb[0:1, :],
                            rlo[0:1, :], start=False, stop=True), inc=ts)
                        ev_rep_prev = ev_rep
                        ev_hl_prev = ev_hl
                        erw = [(ts, ev_rep)]
                        if idx >= 1:
                            erw.append((vs, ev_evacv[idx - 1]))
                        ev_repev_s[idx] = P.op(
                            "scalar", lambda sc: sc.copy(
                                rep_sb[:], PBt.ap().bitcast(F32)[:, 0:512]),
                            waits=erw, inc=ss, sdep=None)
                        ev_evacv[idx] = P.op(
                            "vector", lambda v:
                            v.tensor_tensor(
                                pvT[:, S * h + 512 * c:S * h + 512 * (c + 1)],
                                PB[2 + u2][:, :], rep_sb[:], ALU.mult),
                            waits=[(ss, ev_repev_s[idx]), (ts, lastden)],
                            inc=vs, sdep=None)
                    return tail
                pending_tail = make_tail()
                ev_unit_ts[idx] = g_pvden[step - 1]
                if idx + 4 < len(units):
                    issue_loads(idx + 4)

            if pending_tail is not None:
                pending_tail()
                pending_tail = None
            while dq_i < 64:
                emit_dq_step(dq_i)
                dq_i += 1
            last_exp = g_exp[step - 1]
            ATT_DONE = (vs, ev_evacv[len(units) - 1])
            LAST_EXPS = (ss, last_exp)

            ev_ow = {}
            obank_evac = {}
            ev_rsw = {}
            rs_done = {}
            oi = 0
            for half, cgrp in enumerate([(0, 2), (1, 3)]):
                for fc in range(8):
                    bo = oi % 2
                    w = []
                    if oi >= 2:
                        w.append((ts, ev_ow[oi - 2]))
                    else:
                        w.extend(bar)
                    e_ow = P.dma("gpsimd", qg,
                                 owT[bo].rearrange("p (k w) -> p k w", w=512),
                                 owT_in[:, 512 * fc:512 * (fc + 1)].rearrange(
                                     "(k p) w -> p k w", p=128), waits=w)
                    mm_ev = None
                    for ci in cgrp:
                        for sq_ in range(4):
                            bank = PB[sq_]
                            mw = [(qg, e_ow), ATT_DONE, LAST_EXPS]
                            pr = obank_evac.get(sq_)
                            if pr is not None:
                                mw.append((ss, pr))
                            for k in range(16):
                                def f(t, bank=bank, k=k, ci=ci, sq_=sq_, bo=bo):
                                    ow_ = owT[bo].rearrange(
                                        "p (kk w) -> p kk w", w=512)
                                    return t.matmul(
                                        bank[:],
                                        pvT[:, S * k + 512 * ci + 128 * sq_:
                                            S * k + 512 * ci + 128 * (sq_ + 1)],
                                        ow_[:, k, :],
                                        start=(k == 0), stop=(k == 15))
                                mm_ev = P.op("tensor", f,
                                             waits=mw if k == 0 else (),
                                             inc=ts if k == 15 else None)
                            ew2 = [(ts, mm_ev)]
                            if ("st", sq_) in ev_rsw:
                                ew2.append((qa, ev_rsw[("st", sq_)]))
                            eev = P.op("scalar",
                                       lambda sc, bank=bank, sq_=sq_:
                                       sc.copy(obuf[sq_][:], bank[:]),
                                       waits=ew2, inc=ss, sdep=None)
                            obank_evac[sq_] = eev
                            t0 = 512 * ci + 128 * sq_
                            r0 = (512 * ((t0 % 1024) // 256)
                                  + 256 * (t0 // 1024) + (t0 % 256))
                            ev_rsw[(ci, sq_, fc)] = P.dma(
                                "scalar", qa,
                                rsi_d[r0:r0 + 128, 512 * fc:512 * (fc + 1)],
                                obuf[sq_][:], waits=[(ss, eev)])
                            ev_rsw[("st", sq_)] = ev_rsw[(ci, sq_, fc)]
                    ev_ow[oi] = mm_ev
                    oi += 1
                for g in (2 * half, 2 * half + 1):
                    w_ = [(qa, ev_rsw[(cgrp[1], 3, 7)])]
                    rs_done[g] = P.op(
                        "gpsimd", lambda gp, g=g: gp.collective_compute(
                            "ReduceScatter", ALU.add, replica_groups=pairs,
                            ins=[rsi_d[512 * g:512 * (g + 1), :].opt()],
                            outs=[rso_d[256 * g:256 * (g + 1), :].opt()]),
                        waits=w_, inc=ccol)
            P.emit_phase(block)

        with ExitStack() as s3:
            a_sb = sb(s3, "a_sb", [128, 8 * DM], BF16)
            bias_sb = sb(s3, "bias_sb", [128, DM], F32)
            ssum = sb(s3, "ssum", [128, 4 * 16], F32)
            ssq = sb(s3, "ssq", [128, 4 * 16], F32)
            stat2 = sb(s3, "stat2", [128, 8], F32)

            bar2 = P.barrier_waits()
            ev_bias = P.dma("sync", qs, bias_sb[:], biasr_in[:], waits=bar2)

            ev_a = {}
            if True:
                osum = [sb(s3, f"osum{i}", [128, DM], BF16) for i in range(2)]
                xres = sb(s3, "xres_t", [128, DM], F32)
                r1 = sb(s3, "r1", [128, DM], F32)
                ev_r1 = {}
                ev_sqd = {}
                ev_nrm = {}
                for tt in range(8):
                    g = tt // 2
                    bb = tt % 2
                    w = [(ccol, rs_done[g])] + list(bar2)
                    if tt >= 2:
                        w.append((ss, ev_sqd[tt - 2]))
                    e_os = P.dma("sync", qs, osum[bb][:],
                                 rso_d[128 * tt:128 * (tt + 1), :], waits=w)
                    w2 = [(vs, ev_r1[tt - 1])] if tt >= 1 else []
                    e_xr = P.dma("sync", qs, xres[:],
                                 xres_in[128 * tt:128 * (tt + 1), :], waits=w2)
                    rw = [(qs, e_xr), (qs, e_os)]
                    if tt >= 1:
                        rw.append((vs, ev_nrm[tt - 1]))
                        rw.append((ss, ev_sqd[tt - 1]))
                    ev_r1[tt] = P.op("vector", lambda v, bb=bb:
                                     v.scalar_tensor_tensor(
                                         r1[:], osum[bb][:], 1.0,
                                         xres[:], ALU.mult, ALU.add,
                                         accum_out=stat[:, 8 * bb:8 * bb + 1]),
                                     waits=rw, inc=vs, sdep=None)
                    ev_sqd[tt] = P.op("scalar", lambda sc, bb=bb:
                                      sc.activation(
                                          osum[bb][:], r1[:], AF.Square,
                                          accum_out=stat[:, 8 * bb + 1:
                                                         8 * bb + 2]),
                                      waits=[(vs, ev_r1[tt])], inc=ss,
                                      sdep=None)
                    P.op("vector", lambda v, bb=bb: v.tensor_scalar(
                        stat[:, 8 * bb + 2:8 * bb + 3],
                        stat[:, 8 * bb:8 * bb + 1], 1.0 / DM, None, ALU.mult),
                        sdep=ev_r1[tt])
                    P.op("vector", lambda v, bb=bb: v.tensor_scalar(
                        stat[:, 8 * bb + 3:8 * bb + 4],
                        stat[:, 8 * bb + 1:8 * bb + 2], 1.0 / DM, None,
                        ALU.mult), waits=[(ss, ev_sqd[tt])], sdep=None)
                    P.op("vector", lambda v, bb=bb: v.tensor_tensor(
                        stat[:, 8 * bb + 4:8 * bb + 5],
                        stat[:, 8 * bb + 2:8 * bb + 3],
                        stat[:, 8 * bb + 2:8 * bb + 3], ALU.mult))
                    ev_var = P.op("vector", lambda v, bb=bb: v.tensor_tensor(
                        stat[:, 8 * bb + 5:8 * bb + 6],
                        stat[:, 8 * bb + 3:8 * bb + 4],
                        stat[:, 8 * bb + 4:8 * bb + 5], ALU.subtract), inc=vs)
                    P.op("scalar", lambda sc, bb=bb: sc.activation(
                        stat[:, 8 * bb + 6:8 * bb + 7],
                        stat[:, 8 * bb + 5:8 * bb + 6], AF.Ln,
                        bias=beps[:, 0:1]),
                        waits=[(vs, ev_var)], sdep=None)
                    ev_rstd = P.op("scalar", lambda sc, bb=bb: sc.activation(
                        stat[:, 8 * bb + 7:8 * bb + 8],
                        stat[:, 8 * bb + 6:8 * bb + 7], AF.Exp, scale=-0.5),
                        inc=ss)
                    ev_nrm[tt] = P.op(
                        "vector", lambda v, tt=tt, bb=bb: v.tensor_scalar(
                            a_sb[:, DM * tt:DM * (tt + 1)], r1[:],
                            stat[:, 8 * bb + 2:8 * bb + 3],
                            stat[:, 8 * bb + 7:8 * bb + 8],
                            ALU.subtract, ALU.mult),
                        waits=[(ss, ev_rstd)], inc=vs, sdep=None)
                    ev_a[tt] = ev_nrm[tt]
            P4_DONE = (vs, ev_a[7])
            if dbg:
                evd = P.dma("sync", qs, dbg1_d[:], a_sb[:], waits=[P4_DONE])
                P4_DONE = (qs, evd)

            if True:
                aT = sb(s3, "aT", [128, 32 * 512], BF16)
                wsl = [sb(s3, f"ws{i}", [128, 32 * 256], BF16)
                       for i in range(2)]
                tbuf = sb(s3, "tbuf", [128, 256], F32)
                j256 = sb(s3, "j256", [128, 256], F32)
                outb = [sb(s3, f"outb{i}", [128, DM // 4], F32) for i in range(2)]

                last_out = None
                last_out_ev = {}
                ev_wsl = {}
                ev_fev = {}
                ev_sqf = {}
                ev_trev = None
                wi = 0
                for grp in range(2):
                    for it in range(32):
                        ev_tr = None
                        for tl in range(4):
                            tt = 4 * grp + tl
                            w = [P4_DONE] if dbg else \
                                [(vs, ev_a[4 * grp + 3])]
                            if ev_trev is not None:
                                w.append((ss, ev_trev))
                            ev_tr = P.op(
                                "tensor", lambda t, it=it, tl=tl, tt=tt:
                                t.transpose(PBt[:, 128 * tl:128 * (tl + 1)],
                                            a_sb[:, DM * tt + 128 * it:
                                                 DM * tt + 128 * (it + 1)],
                                            ident[:]),
                                waits=w if tl == 0 else (),
                                inc=ts if tl == 3 else None)
                        ev_trev = P.op("scalar", lambda sc, it=it:
                                       sc.copy(aT[:, 512 * it:512 * (it + 1)],
                                               PBt[:, 0:512]),
                                       waits=[(ts, ev_tr)], inc=ss)
                    AT_RDY = (ss, ev_trev)
                    if dbg and grp == 0:
                        evd2 = P.dma("sync", qs, dbg2_d[:], aT[:],
                                     waits=[AT_RDY])
                        AT_RDY = (qs, evd2)
                    for oc in range(16):
                        bw = oc % 2
                        w = [AT_RDY, (qa, dq_ev_st[63])]
                        if wi >= 2:
                            w.append((ts, ev_wsl[wi - 2]))
                        e_w = P.dma("sync", qs,
                                    wsl[bw].rearrange("p (k w) -> p k w",
                                                      w=256),
                                    wdq_d[oc], waits=w)
                        mm_ev = None
                        for k in range(32):
                            for sq_ in range(4):
                                bank = PB[sq_]
                                mw = []
                                if k == 0:
                                    mw = [(qs, e_w), AT_RDY]
                                    pr = ev_fev.get((wi - 1, sq_))
                                    if pr is not None:
                                        mw.append((vs, pr))
                                def f(t, bank=bank, k=k, sq_=sq_, bw=bw):
                                    wv_ = wsl[bw].rearrange(
                                        "p (kk w) -> p kk w", w=256)
                                    return t.matmul(
                                        bank[:, 0:256],
                                        aT[:, 512 * k + 128 * sq_:
                                           512 * k + 128 * (sq_ + 1)],
                                        wv_[:, k, :],
                                        start=(k == 0), stop=(k == 31))
                                mm_ev = P.op(
                                    "tensor", f, waits=mw,
                                    inc=ts if (k == 31 and sq_ == 3) else None)
                        ev_wsl[wi] = mm_ev
                        for sq_ in range(4):
                            tt = 4 * grp + sq_
                            asl = a_sb[:, DM * tt + 256 * oc:
                                       DM * tt + 256 * (oc + 1)]
                            P.op("vector", lambda v, sq_=sq_, oc=oc:
                                 v.tensor_tensor(
                                     tbuf[:], PB[sq_][:, 0:256],
                                     bias_sb[:, 256 * oc:256 * (oc + 1)],
                                     ALU.add),
                                 waits=[(ts, mm_ev), (qs, ev_bias)],
                                 sdep=None)
                            ev_res = P.op(
                                "vector", lambda v, asl=asl, sq_=sq_, oc=oc:
                                v.scalar_tensor_tensor(
                                    asl, tbuf[:], 1.0, asl, ALU.mult,
                                    ALU.add,
                                    accum_out=ssum[:, 16 * sq_ + oc:
                                                   16 * sq_ + oc + 1]),
                                inc=vs)
                            ev_fev[(wi, sq_)] = ev_res
                            ev_sqf[(wi, sq_)] = P.op(
                                "scalar", lambda sc, asl=asl, sq_=sq_, oc=oc:
                                sc.activation(
                                    j256[:], asl, AF.Square,
                                    accum_out=ssq[:, 16 * sq_ + oc:
                                                  16 * sq_ + oc + 1]),
                                waits=[(vs, ev_res)], inc=ss, sdep=None)
                        wi += 1
                    for sq_ in range(4):
                        tt = 4 * grp + sq_
                        w = [(vs, ev_fev[(wi - 1, sq_)])]
                        P.op("vector", lambda v, sq_=sq_: v.reduce_sum(
                            stat2[:, 0:1], ssum[:, 16 * sq_:16 * (sq_ + 1)],
                            AX), waits=w)
                        P.op("vector", lambda v, sq_=sq_: v.reduce_sum(
                            stat2[:, 1:2], ssq[:, 16 * sq_:16 * (sq_ + 1)],
                            AX), waits=[(ss, ev_sqf[(wi - 1, sq_)])])
                        P.op("vector", lambda v: v.tensor_scalar(
                            stat2[:, 2:3], stat2[:, 0:1], 1.0 / DM, None,
                            ALU.mult))
                        P.op("vector", lambda v: v.tensor_scalar(
                            stat2[:, 3:4], stat2[:, 1:2], 1.0 / DM, None,
                            ALU.mult))
                        P.op("vector", lambda v: v.tensor_tensor(
                            stat2[:, 4:5], stat2[:, 2:3], stat2[:, 2:3],
                            ALU.mult))
                        ev_var = P.op("vector", lambda v: v.tensor_tensor(
                            stat2[:, 5:6], stat2[:, 3:4], stat2[:, 4:5],
                            ALU.subtract), inc=vs)
                        P.op("scalar", lambda sc: sc.activation(
                            stat2[:, 6:7], stat2[:, 5:6], AF.Ln, bias=beps[:, 0:1]),
                            waits=[(vs, ev_var)])
                        ev_rstd = P.op("scalar", lambda sc: sc.activation(
                            stat2[:, 7:8], stat2[:, 6:7], AF.Exp, scale=-0.5),
                            inc=ss)
                        for hf in range(4):
                            bo2 = (4 * tt + hf) % 2
                            ob = outb[bo2]
                            ow = [(ss, ev_rstd)]
                            if last_out_ev.get(bo2) is not None:
                                ow.append((qa, last_out_ev[bo2]))
                            ev_o = P.op(
                                "vector", lambda v, tt=tt, ob=ob, hf=hf:
                                v.tensor_scalar(
                                    ob[:],
                                    a_sb[:, DM * tt + 1024 * hf:
                                         DM * tt + 1024 * (hf + 1)],
                                    stat2[:, 2:3], stat2[:, 7:8],
                                    ALU.subtract, ALU.mult),
                                waits=ow, inc=vs, sdep=None)
                            last_out = P.dma(
                                "scalar", qa,
                                out_ext[128 * tt:128 * (tt + 1),
                                        1024 * hf:1024 * (hf + 1)],
                                ob[:], waits=[(vs, ev_o)])
                            last_out_ev[bo2] = last_out
                fin = qa.n
                fin_qs = qs.n
                P.op("scalar", lambda sc: sc.nop(),
                     waits=[(qa, fin), (qs, fin_qs)])
            P.emit_phase(block)

    return nc


def _rope_tables():
    inv_freq = 1.0 / (10000.0 ** (np.arange(0, D, 2, dtype=np.float64) / D))
    freqs = np.arange(S, dtype=np.float64)[:, None] * inv_freq[None, :]
    emb = np.concatenate([freqs, freqs], axis=-1)
    cos = np.ascontiguousarray(np.cos(emb).T.astype(np.float32))
    sin = np.sin(emb).T.astype(np.float32)
    sgn = np.ones((D, 1), np.float32)
    sgn[:64] = -1.0
    return cos, np.ascontiguousarray(sin * sgn)


def kernel(L0_attn_hidden_states, L0_attn_q_w, L0_attn_k_w, L0_attn_v_w,
           L0_attn_o_w, L0_ffn_weight_packed, L0_ffn_weight_scale,
           L0_ffn_weight_var_mask, L0_ffn_bias):
    x = np.asarray(L0_attn_hidden_states)
    qw = np.asarray(L0_attn_q_w)
    kw = np.asarray(L0_attn_k_w)
    vw = np.asarray(L0_attn_v_w)
    ow = np.asarray(L0_attn_o_w)
    pk = np.asarray(L0_ffn_weight_packed)
    sc = np.asarray(L0_ffn_weight_scale)
    mk = np.asarray(L0_ffn_weight_var_mask)
    bias = np.asarray(L0_ffn_bias)

    perm = np.concatenate([np.arange(0, DM, 2), np.arange(1, DM, 2)])
    cos, sin = _rope_tables()

    p16 = np.ascontiguousarray(pk.view(np.int16)[:, 0::2])
    p16 = p16.reshape(DM, 128, 16).transpose(1, 2, 0).reshape(2048, DM)
    pkT = np.ascontiguousarray(p16[:, perm])
    scP = np.ascontiguousarray(np.repeat(sc.reshape(DM, 128)[perm].T, 16,
                                         axis=0))
    mkP = np.ascontiguousarray(np.repeat(mk.reshape(DM, 128)[perm].T
                                         .view(np.uint8), 16, axis=0))
    biasr = np.ascontiguousarray(np.broadcast_to(bias[perm][None, :],
                                                 (128, DM)))

    in_maps = []
    for c in range(8):
        b, hh = c // 2, c % 2
        rows = slice(2048 * hh, 2048 * (hh + 1))
        in_maps.append({
            "xT": np.ascontiguousarray(x[b].T),
            "xres": np.ascontiguousarray(
                x[b, 1024 * hh:1024 * (hh + 1)][:, perm]),
            "qwT": np.ascontiguousarray(qw[rows].T),
            "kwT": np.ascontiguousarray(kw[rows].T),
            "vwT": np.ascontiguousarray(vw[rows].T),
            "owT": np.ascontiguousarray(ow[perm][:, rows].T),
            "pkT": pkT, "scP": scP, "mkP": mkP, "biasr": biasr,
            "cosD": cos, "sinD": sin,
            "flag": np.array([[1.0 if c < 2 else 0.0]], np.float32),
        })

    nc = build()
    res = run_bass_kernel_spmd(nc, in_maps, core_ids=list(range(8)))

    out = np.empty((4, S, DM), np.float32)
    inv = np.empty(DM, np.int64)
    inv[perm] = np.arange(DM)
    for c in range(8):
        b, hh = c // 2, c % 2
        out[b, 1024 * hh:1024 * (hh + 1)] = res.results[c]["out"][:, inv]
    return out


# version 18 (final)
# speedup vs baseline: 1.1407x; 1.0148x over previous
import numpy as np
import concourse.bass as bass
import concourse.mybir as mybir
from concourse.bass_utils import run_bass_kernel_spmd
from contextlib import ExitStack

F32 = mybir.dt.float32
BF16 = mybir.dt.bfloat16
F16 = mybir.dt.float16
I16 = mybir.dt.int16
U8 = mybir.dt.uint8
ALU = mybir.AluOpType
AF = mybir.ActivationFunctionType
AX = mybir.AxisListType.X

S = 2048
DM = 4096
D = 128
HPC = 16
SCALE = 1.0 / float(np.sqrt(128.0))
EXPB = -20.0
EPS = 1e-5
VIS_LO, VIS_HI = 1, 513
TXT_LO, TXT_HI = 513, 1992


class Sem:
    __slots__ = ("h", "n", "name")

    def __init__(self, h, name):
        self.h = h
        self.n = 0
        self.name = name


class Prog:
    ENGINES = ("tensor", "vector", "scalar", "gpsimd", "sync")

    def __init__(self, nc, stack):
        self.nc = nc
        self.stack = stack
        self.ops = {e: [] for e in self.ENGINES}
        self.last = {e: None for e in self.ENGINES}
        self.auto_serial = {}
        self._serial_prev = {}

    def sem(self, name):
        return Sem(self.stack.enter_context(self.nc.semaphore(name)), name)

    def op(self, eng, fn, waits=(), inc=None, inck=1, is_dma=False,
           sdep="auto"):
        waits = tuple((s, v) for (s, v) in waits if v is not None)
        ser = self.auto_serial.get(eng)
        if ser is not None:
            if sdep == "auto":
                sp = self._serial_prev.get(eng)
                if sp is not None:
                    waits = waits + (sp,)
            elif sdep is not None:
                waits = waits + ((ser, sdep),)
            if not is_dma and inc is None:
                inc = ser
        ev = None
        if inc is not None:
            inc.n += inck
            ev = inc.n
            self.last[eng] = (inc, ev)
            if ser is not None and not is_dma:
                self._serial_prev[eng] = (inc, ev)
        self.ops[eng].append((fn, waits, inc, inck))
        return ev

    def dma(self, q, qsem, out, in_, waits=(), **kw):
        def fn(e, out=out, in_=in_, kw=kw):
            return e.dma_start(out=out, in_=in_, **kw)
        return self.op(q, fn, waits, inc=qsem, inck=16, is_dma=True)

    def barrier_waits(self):
        return tuple(self.last[e] for e in self.ENGINES if self.last[e])

    def emit_phase(self, block):
        for eng in self.ENGINES:
            ops = self.ops[eng]
            if not ops:
                continue
            getattr(block, eng)(self._mk_body(ops))
        self.ops = {e: [] for e in self.ENGINES}

    def _mk_body(self, ops):
        def body(e):
            waited = {}
            for (fn, waits, inc, inck) in ops:
                for (s, v) in waits:
                    if waited.get(id(s), -1) >= v:
                        continue
                    e.wait_ge(s.h, v)
                    waited[id(s)] = v
                inst = fn(e)
                if inc is not None:
                    inst.then_inc(inc.h, inck)
        return body


def build(dbg=False):
    nc = bass.Bass("TRN2", target_bir_lowering=False,
                   detect_race_conditions=False)

    xT_in = nc.dram_tensor("xT", [DM, S], F32, kind="ExternalInput").ap()
    xres_in = nc.dram_tensor("xres", [1024, DM], F32, kind="ExternalInput").ap()
    qwT_in = nc.dram_tensor("qwT", [DM, 2048], F32, kind="ExternalInput").ap()
    kwT_in = nc.dram_tensor("kwT", [DM, 2048], F32, kind="ExternalInput").ap()
    vwT_in = nc.dram_tensor("vwT", [DM, 2048], F32, kind="ExternalInput").ap()
    owT_in = nc.dram_tensor("owT", [2048, DM], F32, kind="ExternalInput").ap()
    pkT_in = nc.dram_tensor("pkT", [2048, DM], I16, kind="ExternalInput").ap()
    scP_in = nc.dram_tensor("scP", [2048, DM], F16, kind="ExternalInput").ap()
    mkP_in = nc.dram_tensor("mkP", [2048, DM], U8, kind="ExternalInput").ap()
    biasr_in = nc.dram_tensor("biasr", [128, DM], F32, kind="ExternalInput").ap()
    cosD_in = nc.dram_tensor("cosD", [128, S], F32, kind="ExternalInput").ap()
    sinD_in = nc.dram_tensor("sinD", [128, S], F32, kind="ExternalInput").ap()
    flag_in = nc.dram_tensor("flag", [1, 1], F32, kind="ExternalInput").ap()
    out_ext = nc.dram_tensor("out", [1024, DM], F32, kind="ExternalOutput").ap()

    qT_d = nc.dram_tensor("qT_d", [HPC, 128, S], BF16).ap()
    kT_d = nc.dram_tensor("kT_d", [HPC, 128, S], BF16).ap()
    v_d = nc.dram_tensor("v_d", [S, 2048], BF16).ap()
    wdq_d = nc.dram_tensor("wdq_d", [16, 128, 32, 256], BF16).ap()
    rsi_d = nc.dram_tensor("rsi_d", [S, DM], BF16).ap()
    rso_d = nc.dram_tensor("rso_d", [1024, DM], BF16).ap()
    impi_d = nc.dram_tensor("impi_d", [1, 512], F32).ap()
    impo_d = nc.dram_tensor("impo_d", [1, 512], F32).ap()
    mask_d = nc.dram_tensor("mask_d", [1, 768], F32).ap()
    dbgkind = "ExternalOutput" if dbg == 2 else "Internal"
    dbg1_d = nc.dram_tensor("dbg1_d", [128, 8 * DM], BF16, kind=dbgkind).ap() \
        if dbg else None
    dbg2_d = nc.dram_tensor("dbg2_d", [128, 32 * 512], BF16, kind=dbgkind).ap() \
        if dbg else None
    dbg3_d = nc.dram_tensor("dbg3_d", [128, 8 * DM], BF16, kind=dbgkind).ap() \
        if dbg else None
    if dbg == 2:
        dbg_qT = nc.dram_tensor("dbg_qT", [HPC, 128, S], BF16, kind=dbgkind).ap()
        dbg_kT = nc.dram_tensor("dbg_kT", [HPC, 128, S], BF16, kind=dbgkind).ap()
        dbg_v = nc.dram_tensor("dbg_v", [S, 2048], BF16, kind=dbgkind).ap()
        dbg_w = nc.dram_tensor("dbg_w", [16, 128, 32, 256], BF16, kind=dbgkind).ap()
        dbg_rsi = nc.dram_tensor("dbg_rsi", [S, DM], BF16, kind=dbgkind).ap()
        dbg_rso = nc.dram_tensor("dbg_rso", [1024, DM], BF16, kind=dbgkind).ap()
        dbg_msk = nc.dram_tensor("dbg_msk", [1, 768], F32, kind=dbgkind).ap()

    pairs = [[0, 1], [2, 3], [4, 5], [6, 7]]
    allg = [[0, 1, 2, 3, 4, 5, 6, 7]]

    with ExitStack() as stack:
        P = Prog(nc, stack)
        qg = P.sem("qg")
        qs = P.sem("qs")
        qa = P.sem("qa")
        ccol = P.sem("ccol")
        ts = P.sem("ts")
        vs = P.sem("vs")
        ss = P.sem("ss")
        gs = P.sem("gs")
        P.auto_serial = {"vector": vs, "scalar": ss}

        PB = [stack.enter_context(nc.psum_tensor(f"pb{i}", [128, 512], F32))
              for i in range(7)]
        PBt = stack.enter_context(nc.psum_tensor("pbt", [128, 1024], BF16))

        sb = lambda st_, n, shp, dt: st_.enter_context(nc.sbuf_tensor(n, shp, dt))
        ident = sb(stack, "ident", [128, 128], BF16)
        tri = sb(stack, "tri", [128, 128], BF16)
        maskb = sb(stack, "maskb", [128, 8], F32)
        flag_sb = sb(stack, "flag_sb", [1, 1], F32)
        ones_r = sb(stack, "ones_r", [1, 128], F32)
        one1 = sb(stack, "one1", [1, 1], F32)
        stat = sb(stack, "stat", [128, 16], F32)
        b20 = sb(stack, "b20", [128, 1], F32)
        beps = sb(stack, "beps", [128, 1], F32)

        block = stack.enter_context(nc.Block())

        ev_flag = P.dma("sync", qs, flag_sb[:], flag_in[:])
        e1 = P.op("gpsimd", lambda g: g.memset(ident[:], 0.0), inc=gs)
        e2 = P.op("gpsimd", lambda g: g.affine_select(
            out=ident[:], in_=ident[:], compare_op=ALU.not_equal, fill=1.0,
            base=0, pattern=[[-1, 128]], channel_multiplier=1),
            waits=[(gs, e1)], inc=gs)
        e3 = P.op("gpsimd", lambda g: g.memset(tri[:], 1.0),
                  waits=[(gs, e2)], inc=gs)
        ev_tri = P.op("gpsimd", lambda g: g.affine_select(
            out=tri[:], in_=tri[:], compare_op=ALU.is_ge, fill=0.0,
            base=0, pattern=[[1, 128]], channel_multiplier=-1),
            waits=[(gs, e3)], inc=gs)
        P.op("vector", lambda v: v.memset(ones_r[:], 1.0))
        P.op("vector", lambda v: v.memset(b20[:], EXPB))
        P.op("vector", lambda v: v.memset(beps[:], EPS))
        ev_one = P.op("vector", lambda v: v.memset(one1[:], 1.0), inc=vs)
        P.emit_phase(block)

        MASK_READY = None
        with ExitStack() as s1:
            xT = sb(s1, "xT_sb", [128, 32 * S], BF16)
            wb = [sb(s1, f"wb{i}", [128, 32 * 256], BF16) for i in range(2)]
            cos_sb = sb(s1, "cos_sb", [128, 512], F32)
            sin_sb = sb(s1, "sin_sb", [128, 512], F32)
            kraw = sb(s1, "kraw", [128, 1024], F32)
            kp = sb(s1, "kp", [128, 1024], F32)
            robuf = sb(s1, "robuf", [128, 1024], BF16)
            redt = sb(s1, "redt", [128, 1], F32)
            vout = sb(s1, "vout", [128, 256], BF16)
            sq16 = sb(s1, "sq16", [128, 16], F32)
            sq16b = sb(s1, "sq16b", [128, 16], BF16)
            kvisb = sb(s1, "kvisb", [128, 16 * 512], BF16)
            impf = sb(s1, "impf", [1, 512], F32)
            impfh = sb(s1, "impfh", [1, 512], BF16)
            impfl = sb(s1, "impfl", [1, 512], F32)
            impflb = sb(s1, "impflb", [1, 512], BF16)
            ones_rb1 = sb(s1, "ones_rb1", [1, 128], BF16)
            one1b = sb(s1, "one1b", [1, 1], BF16)
            impB = sb(s1, "impB", [128, 512], F32)
            impP = sb(s1, "impP", [128, 4], F32)
            gt512 = sb(s1, "gt512", [128, 512], F32)
            rank4 = sb(s1, "rank4", [128, 4], F32)
            m768 = sb(s1, "m768", [1, 768], F32)

            xTv = xT.rearrange("p (k t) -> p k t", t=S)
            wbv = [w.rearrange("p (k w) -> p k w", w=256) for w in wb]

            P.op("vector", lambda v: v.memset(ones_rb1[:], 1.0))
            P.op("vector", lambda v: v.memset(one1b[:], 1.0))
            ev_sq0 = P.op("vector", lambda v: v.memset(sq16[:], 0.0), inc=vs)

            ev_xt = []
            ev_xt.append(P.dma(
                "gpsimd", qg, xTv[:, :, 0:512],
                xT_in[:, 0:512].rearrange("(k p) t -> p k t", p=128)))

            ev_wload = {}
            proj_evac = []
            ev_t2 = None
            ev_addk = None
            ev_robf = None
            ev_shift = None
            ev_sq = ev_sq0
            last_qs_store = None
            n_unit = 0
            v_unit = 0
            ev_vout_dma = None
            ev_vout_evac = []
            gi = 0

            for pi, wsrc in enumerate([qwT_in, kwT_in, vwT_in]):
                dst = [qT_d, kT_d, None][pi]
                for g in range(8):
                    waits = []
                    if gi >= 2:
                        waits.append((ts, ev_wload[gi - 2]))
                    ev_w = P.dma("gpsimd", qg, wbv[gi % 2][:, :, :],
                                 wsrc[:, 256 * g:256 * (g + 1)].rearrange(
                                     "(k p) w -> p k w", p=128), waits=waits)
                    if gi == 0:
                        for c_ in range(1, 4):
                            ev_xt.append(P.dma(
                                "gpsimd", qg,
                                xTv[:, :, 512 * c_:512 * (c_ + 1)],
                                xT_in[:, 512 * c_:512 * (c_ + 1)].rearrange(
                                    "(k p) t -> p k t", p=128)))
                    mm_ev = None
                    if pi < 2:
                        for c in range(4):
                            cw = [(qg, ev_w), (qg, ev_xt[c])]
                            if ev_t2 is not None:
                                cw.append((vs, ev_t2))
                            evc = P.dma("sync", qs, cos_sb[:],
                                        cosD_in[:, 512 * c:512 * (c + 1)],
                                        waits=cw)
                            evc2 = P.dma("sync", qs, sin_sb[:],
                                         sinD_in[:, 512 * c:512 * (c + 1)])
                            for hh in range(2):
                                bank = PB[n_unit % 2]
                                mw = [(qg, ev_w), (qg, ev_xt[c])]
                                if n_unit >= 2:
                                    mw.append((ss, proj_evac[n_unit - 2]))
                                for k in range(32):
                                    def f(t, bank=bank, k=k, hh=hh, c=c,
                                          wv=wbv[gi % 2]):
                                        return t.matmul(
                                            bank[:],
                                            wv[:, k, 128 * hh:128 * (hh + 1)],
                                            xTv[:, k, 512 * c:512 * (c + 1)],
                                            start=(k == 0), stop=(k == 31))
                                    mm_ev = P.op("tensor", f,
                                                 waits=mw if k == 0 else (),
                                                 inc=ts if k == 31 else None)
                                ew = [(ts, mm_ev)]
                                if ev_addk is not None:
                                    ew.append((vs, ev_addk))
                                if ev_shift is not None:
                                    ew.append((qs, ev_shift))
                                eev = P.op(
                                    "scalar", lambda sc, bank=bank, hh=hh:
                                    sc.copy(kraw[:, 512 * hh:512 * (hh + 1)],
                                            bank[:]), waits=ew, inc=ss,
                                    sdep=None)
                                proj_evac.append(eev)
                                n_unit += 1
                            shw = [(ss, proj_evac[-1])]
                            if ev_robf is not None:
                                shw.append((vs, ev_robf))
                            P.dma("sync", qs, kp[0:64, :], kraw[64:128, :],
                                  waits=shw)
                            ev_shift = P.dma("sync", qs, kp[64:128, :],
                                             kraw[0:64, :])
                            k3 = kraw.rearrange("p (h t) -> p h t", t=512)
                            p3 = kp.rearrange("p (h t) -> p h t", t=512)
                            cb = cos_sb[:].unsqueeze(1).broadcast_to([128, 2, 512])
                            sbx = sin_sb[:].unsqueeze(1).broadcast_to([128, 2, 512])
                            P.op("vector", lambda v, k3=k3, cb=cb:
                                 v.tensor_tensor(k3, k3, cb, ALU.mult),
                                 waits=[(qs, evc2), (qs, ev_shift)])
                            ev_t2 = P.op("vector", lambda v, p3=p3, sbx=sbx:
                                         v.tensor_tensor(p3, p3, sbx, ALU.mult),
                                         inc=vs)
                            rw = []
                            if last_qs_store is not None:
                                rw.append((qs, last_qs_store))
                            r3 = robuf.rearrange("p (h t) -> p h t", t=512)
                            ev_addk = P.op("vector", lambda v, r3=r3, k3=k3, p3=p3:
                                           v.tensor_tensor(r3, k3, p3, ALU.add),
                                           waits=rw, inc=vs)
                            ev_robf = ev_addk
                            if pi == 0:
                                tlo = max(TXT_LO, 512 * c) - 512 * c
                                thi = min(TXT_HI, 512 * (c + 1)) - 512 * c
                                if thi > tlo:
                                    for hh in range(2):
                                        hcol = 2 * g + hh
                                        P.op("vector",
                                             lambda v, hh=hh, tlo=tlo, thi=thi:
                                             v.reduce_sum(
                                                 redt[:],
                                                 robuf[:, 512 * hh + tlo:
                                                       512 * hh + thi], AX))
                                        ev_sq = P.op(
                                            "vector", lambda v, hcol=hcol:
                                            v.tensor_tensor(
                                                sq16[:, hcol:hcol + 1],
                                                sq16[:, hcol:hcol + 1],
                                                redt[:], ALU.add), inc=vs)
                                    ev_robf = ev_sq
                            for hh in range(2):
                                h = 2 * g + hh
                                last_qs_store = P.dma(
                                    "sync", qs,
                                    dst[h, :, 512 * c:512 * (c + 1)],
                                    robuf[:, 512 * hh:512 * (hh + 1)],
                                    waits=[(vs, ev_addk)])
                                if pi == 1 and c == 1:
                                    ev_kvl = P.dma(
                                        "sync", qs,
                                        kvisb[:, 512 * h:512 * (h + 1)],
                                        kT_d[h, :, VIS_LO:VIS_HI])
                    else:
                        for c in range(4):
                            for sub in range(4):
                                tt = 4 * c + sub
                                bank = PB[v_unit % 2]
                                mw = [(qg, ev_w), (qg, ev_xt[c])]
                                if v_unit >= 2:
                                    mw.append((ss, ev_vout_evac[v_unit - 2]))
                                for k in range(32):
                                    def f(t, bank=bank, k=k, tt=tt,
                                          wv=wbv[gi % 2]):
                                        return t.matmul(
                                            bank[:, 0:256],
                                            xTv[:, k, 128 * tt:128 * (tt + 1)],
                                            wv[:, k, :],
                                            start=(k == 0), stop=(k == 31))
                                    mm_ev = P.op("tensor", f,
                                                 waits=mw if k == 0 else (),
                                                 inc=ts if k == 31 else None)
                                ew = [(ts, mm_ev)]
                                if ev_vout_dma is not None:
                                    ew.append((qs, ev_vout_dma))
                                eev = P.op("scalar", lambda sc, bank=bank:
                                           sc.copy(vout[:], bank[:, 0:256]),
                                           waits=ew, inc=ss, sdep=None)
                                ev_vout_evac.append(eev)
                                ev_vout_dma = P.dma(
                                    "sync", qs,
                                    v_d[128 * tt:128 * (tt + 1),
                                        256 * g:256 * (g + 1)],
                                    vout[:], waits=[(ss, eev)])
                                v_unit += 1
                    ev_wload[gi] = mm_ev
                    gi += 1

                if pi == 1:
                    ev_k_st = last_qs_store
                    ev_sqb = P.op("vector",
                                  lambda v: v.tensor_copy(sq16b[:], sq16[:]),
                                  waits=[(vs, ev_sq)], inc=vs)
                    impev = None
                    for h in range(HPC):
                        def f(t, h=h):
                            return t.matmul(
                                PB[4][0:1, :], sq16b[:, h:h + 1],
                                kvisb[:, 512 * h:512 * (h + 1)],
                                start=(h == 0), stop=(h == HPC - 1))
                        w2 = [(qs, ev_kvl), (vs, ev_sqb)] if h == 0 else []
                        impev = P.op("tensor", f, waits=w2,
                                     inc=ts if h == HPC - 1 else None)
                    ev_if = P.op("vector", lambda v: v.tensor_scalar(
                        impf[:], PB[4][0:1, :], flag_sb[0:1, 0:1], None,
                        ALU.mult), waits=[(ts, impev), (qs, ev_flag)], inc=vs,
                        sdep=None)
                    ev_iw = P.dma("sync", qs, impi_d[:], impf[:],
                                  waits=[(vs, ev_if)])
                    ev_ar = P.op("gpsimd", lambda g: g.collective_compute(
                        "AllReduce", ALU.add, replica_groups=allg,
                        ins=[impi_d.opt()], outs=[impo_d.opt()]),
                        waits=[(qs, ev_iw)], inc=ccol)
                    ev_ir = P.dma("gpsimd", qg, impf[:], impo_d[:],
                                  waits=[(ccol, ev_ar)])
                    saved_ev_ir = ev_ir
                    saved_ev_if = ev_if

            if True:
                    ev_ir = saved_ev_ir
                    ev_if = saved_ev_if
                    ev_hi = P.op("vector", lambda v: v.tensor_copy(
                        impfh[:], impf[:]),
                        waits=[(qg, ev_ir), (vs, ev_if)], sdep=None)
                    P.op("vector", lambda v: v.tensor_tensor(
                        impfl[:], impf[:], impfh[:], ALU.subtract))
                    ev_hl2 = P.op("vector", lambda v: v.tensor_copy(
                        impflb[:], impfl[:]), inc=vs)
                    P.op("tensor", lambda t: t.matmul(
                        PB[5][:, :], ones_rb1[0:1, :], impfh[0:1, :],
                        start=True, stop=False),
                        waits=[(vs, ev_hl2), (vs, ev_one)])
                    ev_ib = P.op("tensor", lambda t: t.matmul(
                        PB[5][:, :], ones_rb1[0:1, :], impflb[0:1, :],
                        start=False, stop=True), inc=ts)
                    ev_ipm = None
                    for qq in range(4):
                        def f(t, qq=qq):
                            return t.matmul(PB[6][:, qq:qq + 1],
                                            impfh[0:1, 128 * qq:128 * (qq + 1)],
                                            one1b[0:1, 0:1],
                                            start=True, stop=False)
                        P.op("tensor", f)
                        def f2(t, qq=qq):
                            return t.matmul(PB[6][:, qq:qq + 1],
                                            impflb[0:1, 128 * qq:128 * (qq + 1)],
                                            one1b[0:1, 0:1],
                                            start=False, stop=True)
                        ev_ipm = P.op("tensor", f2,
                                      inc=ts if qq == 3 else None)
                    ev_ibc = P.op("scalar", lambda sc: sc.copy(impB[:], PB[5][:]),
                                  waits=[(ts, ev_ib)], inc=ss, sdep=None)
                    ev_ipc = P.op("scalar",
                                  lambda sc: sc.copy(impP[:], PB[6][:, 0:4]),
                                  waits=[(ts, ev_ipm)], inc=ss, sdep=None)
                    for qq in range(4):
                        P.op("vector", lambda v, qq=qq: v.tensor_scalar(
                            gt512[:], impB[:], impP[:, qq:qq + 1], None,
                            ALU.is_gt),
                            waits=[(ss, ev_ipc), (ss, ev_ibc)], sdep=None)
                        P.op("vector", lambda v, qq=qq: v.reduce_sum(
                            rank4[:, qq:qq + 1], gt512[:], AX))
                    P.op("vector", lambda v: v.tensor_scalar(
                        rank4[:], rank4[:], 255.5, None, ALU.is_lt))
                    P.op("vector", lambda v: v.tensor_scalar(
                        rank4[:], rank4[:], 1.0e30, -1.0e30, ALU.mult, ALU.add))
                    ev_mv = P.op("vector", lambda v: v.tensor_scalar(
                        rank4[:], rank4[:], -20.0, None, ALU.add), inc=vs)
                    ev_m20 = P.op("vector", lambda v: v.memset(m768[:], -20.0),
                                  inc=vs)
                    ev_md = P.dma("sync", qs, mask_d[:], m768[:],
                                  waits=[(vs, ev_m20)])
                    ev_md2 = P.dma(
                        "sync", qs,
                        mask_d[0:1, 1:513].rearrange("a (q p) -> (a p) q", p=128),
                        rank4[:], waits=[(vs, ev_mv), (qs, ev_md)],
                        allow_slow_non_contiguous=True)
                    ev_mb = None
                    for j in range(5):
                        ev_mb = P.dma("sync", qs, maskb[:, j:j + 1],
                                      mask_d[0:1, 128 * j:128 * (j + 1)]
                                      .rearrange("a p -> p a"),
                                      waits=[(qs, ev_md2)])
                    MASK_READY = (qs, ev_mb)

            LAST_V = (qs, ev_vout_dma)
            P.emit_phase(block)

        with ExitStack() as s2:
            pvT = sb(s2, "pvT", [128, HPC * S], BF16)
            qbuf = [sb(s2, f"qb{i}", [128, 512], BF16) for i in range(4)]
            kbuf = [sb(s2, f"kb{i}", [128, 2048], BF16) for i in range(4)]
            vbuf = [sb(s2, f"vb{i}", [128, 16 * 128], BF16) for i in range(4)]
            pT = [sb(s2, f"pt{i}", [128, 512], BF16) for i in range(4)]
            rcp1 = sb(s2, "rcp1", [1, 512], F32)
            den_sb = sb(s2, "den_sb", [1, 512], F32)
            lnb = sb(s2, "lnb", [1, 512], F32)
            rhi = sb(s2, "rhi", [1, 512], BF16)
            rlo = sb(s2, "rlo", [1, 512], BF16)
            rlf = sb(s2, "rlf", [1, 512], F32)
            ones_rb = sb(s2, "ones_rb", [1, 128], BF16)
            rep_sb = sb(s2, "rep_sb", [128, 512], F32)
            ones_bf = sb(s2, "ones_bf", [128, 1], BF16)
            dq_pk = [sb(s2, f"dqp{i}", [128, 1024], I16) for i in range(2)]
            dq_sc = sb(s2, "dqs", [128, 1024], F32)
            dq_mk = sb(s2, "dqm", [128, 1024], F32)
            dq_lo = sb(s2, "dql", [128, 1024], I16)
            dq_w = [sb(s2, f"dqw{i}", [128, 1024], BF16) for i in range(2)]
            owT = [sb(s2, f"ow{i}", [128, 16 * 512], BF16) for i in range(2)]
            obuf = [sb(s2, f"ob{i}", [128, 512], BF16) for i in range(4)]

            bar = P.barrier_waits()
            P.op("vector", lambda v: v.memset(ones_rb[:], 1.0), waits=bar)
            ev_vb = P.op("vector", lambda v: v.memset(ones_bf[:], 1.0),
                         inc=vs)

            dq_ev_v = {}
            dq_ev_st = {}

            def emit_dq_step(i):
                pt_i, ch = i // 4, i % 4
                cols = slice(1024 * ch, 1024 * (ch + 1))
                b = i % 2
                w_pk = list(bar) if i == 0 else []
                if i >= 2:
                    w_pk.append((vs, dq_ev_v[i - 2]))
                e_pk = P.dma("sync", qs, dq_pk[b][:],
                             pkT_in[128 * pt_i:128 * (pt_i + 1), cols],
                             waits=w_pk)
                w_sc = list(bar) if i == 0 else [(vs, dq_ev_v[i - 1])]
                e_sc = P.dma("gpsimd", qg, dq_sc[:],
                             scP_in[128 * pt_i:128 * (pt_i + 1), cols],
                             waits=w_sc)
                e_mk = P.dma("gpsimd", qg, dq_mk[:],
                             mkP_in[128 * pt_i:128 * (pt_i + 1), cols])
                e_me = P.op("gpsimd", lambda g: g.tensor_scalar(
                    dq_mk[:], dq_mk[:], 0.5, 0.5, ALU.mult, ALU.add),
                    waits=[(qg, e_mk)], inc=gs)
                e_se = P.op("gpsimd", lambda g: g.tensor_tensor(
                    dq_mk[:], dq_sc[:], dq_mk[:], ALU.mult),
                    waits=[(qg, e_sc), (gs, e_me)], inc=gs)
                w_w = [(qa, dq_ev_st[i - 1])] if i >= 1 else []
                e_an = P.op("vector", lambda v, b=b: v.tensor_scalar(
                    dq_lo[:], dq_pk[b][:], 15, None, ALU.bitwise_and),
                    waits=[(qs, e_pk)], inc=vs, sdep=None)
                e_sh = P.op("vector", lambda v, b=b: v.tensor_scalar(
                    dq_pk[b][:], dq_pk[b][:], 4, None,
                    ALU.logical_shift_right), inc=vs, sdep=None)
                P.op("vector", lambda v: v.scalar_tensor_tensor(
                    dq_w[0][:], dq_lo[:], -8.0, dq_mk[:], ALU.add, ALU.mult),
                    waits=w_w + [(gs, e_se)], sdep=e_an)
                ev_v = P.op("vector", lambda v, b=b: v.scalar_tensor_tensor(
                    dq_w[1][:], dq_pk[b][:], -8.0, dq_mk[:], ALU.add, ALU.mult),
                    inc=vs, sdep=e_sh)
                dq_ev_v[i] = ev_v
                P.dma("scalar", qa,
                      wdq_d[4 * ch:4 * (ch + 1), :, pt_i, :]
                      .transpose([1, 0, 2]),
                      dq_w[0].rearrange("p (o w) -> p o w", w=256),
                      waits=[(vs, ev_v)])
                dq_ev_st[i] = P.dma(
                    "scalar", qa,
                    wdq_d[4 * ch:4 * (ch + 1), :, 16 + pt_i, :]
                    .transpose([1, 0, 2]),
                    dq_w[1].rearrange("p (o w) -> p o w", w=256))

            units = [(h, c) for c in (3, 2, 1, 0) for h in range(HPC)]
            ev_qload = {}
            ev_kload = {}
            ev_vload = {}
            ev_unit_ts = {}
            g_exp = {}
            g_pvden = {}
            ev_evacv = {}
            ev_repev = {}
            ev_repev_s = {}
            ev_rcd = {}
            step = 0
            dq_i = 0

            def issue_loads(idx):
                h, c = units[idx]
                b = idx % 4
                J = 4 * c + 4
                w = []
                if idx >= 4:
                    w.append((ts, ev_unit_ts[idx - 4]))
                else:
                    w.extend(bar)
                    w.append(LAST_V)
                ev_qload[idx] = P.dma("sync", qs, qbuf[b][:],
                                      qT_d[h, :, 512 * c:512 * (c + 1)],
                                      waits=w)
                ev_kload[idx] = P.dma("sync", qs, kbuf[b][:, 0:128 * J],
                                      kT_d[h, :, 0:128 * J])
                vv = vbuf[b].rearrange("p (j w) -> p j w", w=128)
                ev_vload[idx] = P.dma(
                    "sync", qs, vv[:, 0:J, :],
                    v_d[0:128 * J, 128 * h:128 * (h + 1)].rearrange(
                        "(j p) w -> p j w", p=128))

            for _i in range(4):
                issue_loads(_i)
            pending_tail = None
            pend = []
            ev_rep_prev = None
            ev_hl_prev = None
            for idx, (h, c) in enumerate(units):
                b = idx % 4
                u2 = idx % 2
                J = 4 * c + 4
                korder = (list(range(5, J)) + [0, 1, 2, 3, 4])[:J] if J > 4 \
                    else list(range(J))
                vv = vbuf[b].rearrange("p (j w) -> p j w", w=128)
                nk = len(korder)

                def emit_pvden(sp, jn, j, ev_vx):
                    pw = [(ss, g_exp[sp]), (qs, ev_vload[idx])]
                    if ev_vx is not None:
                        pw.append((vs, ev_vx))
                    if jn == 0:
                        if idx >= 2:
                            pw.append((vs, ev_evacv[idx - 2]))
                        else:
                            pw.append((vs, ev_if))
                    P.op("tensor", lambda t, sp=sp, j=j, jn=jn, vv=vv,
                         u2=u2, nk=nk:
                         t.matmul(PB[2 + u2][:, :], vv[:, j, :],
                                  pT[sp % 4][:, :],
                                  start=(jn == 0), stop=(jn == nk - 1)),
                         waits=pw)
                    dw = []
                    if jn == 0:
                        dw.append((vs, ev_vb))
                        if idx >= 2:
                            dw.append((ss, ev_rcd[idx - 2]))
                        elif idx == 0:
                            dw.append((vs, ev_if))
                        else:
                            dw.append((ss, ev_ibc))
                    g_pvden[sp] = P.op(
                        "tensor", lambda t, sp=sp, jn=jn, u2=u2, nk=nk:
                        t.matmul(PB[4 + u2][0:1, :], ones_bf[:, 0:1],
                                 pT[sp % 4][:, :],
                                 start=(jn == 0), stop=(jn == nk - 1)),
                        waits=dw, inc=ts)

                for jn, j in enumerate(korder):
                    ab = [PB[0], PB[1], PB[6]][step % 3]
                    aw = [(qs, ev_kload[idx]), (qs, ev_qload[idx])]
                    if step >= 3:
                        aw.append((ss, g_exp[step - 3]))
                    ev_att = P.op(
                        "tensor", lambda t, ab=ab, j=j, b=b:
                        t.matmul(ab[:], kbuf[b][:, 128 * j:128 * (j + 1)],
                                 qbuf[b][:], start=True, stop=True),
                        waits=aw, inc=ts)
                    d = j - 4 * c
                    isdiag = 0 <= d <= 3
                    col0 = 128 * d if isdiag else 0
                    ew = [(ts, ev_att)]
                    if step >= 4:
                        ew.append((ts, g_pvden[step - 4]))
                    if j <= 4:
                        ew.append(MASK_READY)
                    bias = maskb[:, j:j + 1] if j <= 4 else b20[:, 0:1]
                    g_exp[step] = P.op(
                        "scalar", lambda sc, sp=step, ab=ab, col0=col0,
                        bias=bias:
                        sc.activation(pT[sp % 4][:, col0:512], ab[:, col0:512],
                                      AF.Exp, bias=bias, scale=SCALE),
                        waits=ew, inc=ss, sdep=None)
                    ev_vx = None
                    if isdiag and d > 0:
                        P.op("vector", lambda v, sp=step, col0=col0:
                             v.memset(pT[sp % 4][:, 0:col0], 0.0),
                             waits=[(ss, g_exp[step])], sdep=None)
                    if isdiag:
                        ev_vx = P.op(
                            "vector", lambda v, sp=step, d=d:
                            v.tensor_tensor(
                                pT[sp % 4][:, 128 * d:128 * (d + 1)],
                                pT[sp % 4][:, 128 * d:128 * (d + 1)],
                                tri[:], ALU.mult),
                            waits=[(ss, g_exp[step]), (gs, ev_tri)], inc=vs,
                            sdep=None)
                    if len(pend) == 3:
                        emit_pvden(*pend.pop(0))
                    pend.append((step, jn, j, ev_vx))
                    step += 1
                    if jn == 3 and pending_tail is not None:
                        pending_tail()
                        pending_tail = None
                        if dq_i < 64:
                            emit_dq_step(dq_i)
                            dq_i += 1
                while pend:
                    emit_pvden(*pend.pop(0))
                def make_tail(idx=idx, h=h, c=c, u2=u2, lastden=g_pvden[step - 1]):
                    def tail():
                        nonlocal ev_rep_prev, ev_hl_prev
                        dew = [(ts, lastden)]
                        if idx >= 1:
                            dew.append((vs, ev_hl_prev))
                        ev_de = P.op("scalar", lambda sc:
                                     sc.copy(den_sb[:], PB[4 + u2][0:1, :]),
                                     waits=dew, inc=ss, sdep=None)
                        ev_rcd[idx] = ev_de
                        P.op("scalar", lambda sc: sc.activation(
                            lnb[:], den_sb[:], AF.Ln), sdep=ev_de)
                        ev_rc = P.op("scalar", lambda sc: sc.activation(
                            rcp1[:], lnb[:], AF.Exp, scale=-1.0), inc=ss)
                        hw_ = [(ss, ev_rc)]
                        if idx >= 1:
                            hw_.append((ts, ev_rep_prev))
                        P.op("vector", lambda v: v.tensor_copy(rhi[:], rcp1[:]),
                             waits=hw_, sdep=None)
                        P.op("vector", lambda v: v.tensor_tensor(
                            rlf[:], rcp1[:], rhi[:], ALU.subtract))
                        ev_hl = P.op("vector",
                                     lambda v: v.tensor_copy(rlo[:], rlf[:]),
                                     inc=vs)
                        rw = [(vs, ev_hl)]
                        if idx >= 1:
                            rw.append((ss, ev_repev_s[idx - 1]))
                        else:
                            rw.append((ss, ev_ipc))
                        P.op("tensor", lambda t: t.matmul(
                            PBt.ap().bitcast(F32)[:, 0:512], ones_rb[0:1, :],
                            rhi[0:1, :], start=True, stop=False), waits=rw)
                        ev_rep = P.op("tensor", lambda t: t.matmul(
                            PBt.ap().bitcast(F32)[:, 0:512], ones_rb[0:1, :],
                            rlo[0:1, :], start=False, stop=True), inc=ts)
                        ev_rep_prev = ev_rep
                        ev_hl_prev = ev_hl
                        erw = [(ts, ev_rep)]
                        if idx >= 1:
                            erw.append((vs, ev_evacv[idx - 1]))
                        ev_repev_s[idx] = P.op(
                            "scalar", lambda sc: sc.copy(
                                rep_sb[:], PBt.ap().bitcast(F32)[:, 0:512]),
                            waits=erw, inc=ss, sdep=None)
                        ev_evacv[idx] = P.op(
                            "vector", lambda v:
                            v.tensor_tensor(
                                pvT[:, S * h + 512 * c:S * h + 512 * (c + 1)],
                                PB[2 + u2][:, :], rep_sb[:], ALU.mult),
                            waits=[(ss, ev_repev_s[idx]), (ts, lastden)],
                            inc=vs, sdep=None)
                    return tail
                pending_tail = make_tail()
                ev_unit_ts[idx] = g_pvden[step - 1]
                if idx + 4 < len(units):
                    issue_loads(idx + 4)

            if pending_tail is not None:
                pending_tail()
                pending_tail = None
            while dq_i < 64:
                emit_dq_step(dq_i)
                dq_i += 1
            last_exp = g_exp[step - 1]
            ATT_DONE = (vs, ev_evacv[len(units) - 1])
            LAST_EXPS = (ss, last_exp)

            ev_ow = {}
            obank_evac = {}
            ev_rsw = {}
            rs_done = {}
            oi = 0
            for half, cgrp in enumerate([(0, 2), (1, 3)]):
                for fc in range(8):
                    bo = oi % 2
                    w = []
                    if oi >= 2:
                        w.append((ts, ev_ow[oi - 2]))
                    else:
                        w.extend(bar)
                    e_ow = P.dma("gpsimd", qg,
                                 owT[bo].rearrange("p (k w) -> p k w", w=512),
                                 owT_in[:, 512 * fc:512 * (fc + 1)].rearrange(
                                     "(k p) w -> p k w", p=128), waits=w)
                    mm_ev = None
                    for ci in cgrp:
                        for sq_ in range(4):
                            bank = PB[sq_]
                            mw = [(qg, e_ow), ATT_DONE, LAST_EXPS]
                            pr = obank_evac.get(sq_)
                            if pr is not None:
                                mw.append((ss, pr))
                            for k in range(16):
                                def f(t, bank=bank, k=k, ci=ci, sq_=sq_, bo=bo):
                                    ow_ = owT[bo].rearrange(
                                        "p (kk w) -> p kk w", w=512)
                                    return t.matmul(
                                        bank[:],
                                        pvT[:, S * k + 512 * ci + 128 * sq_:
                                            S * k + 512 * ci + 128 * (sq_ + 1)],
                                        ow_[:, k, :],
                                        start=(k == 0), stop=(k == 15))
                                mm_ev = P.op("tensor", f,
                                             waits=mw if k == 0 else (),
                                             inc=ts if k == 15 else None)
                            ew2 = [(ts, mm_ev)]
                            if ("st", sq_) in ev_rsw:
                                ew2.append((qa, ev_rsw[("st", sq_)]))
                            eev = P.op("scalar",
                                       lambda sc, bank=bank, sq_=sq_:
                                       sc.copy(obuf[sq_][:], bank[:]),
                                       waits=ew2, inc=ss, sdep=None)
                            obank_evac[sq_] = eev
                            t0 = 512 * ci + 128 * sq_
                            r0 = (512 * ((t0 % 1024) // 256)
                                  + 256 * (t0 // 1024) + (t0 % 256))
                            ev_rsw[(ci, sq_, fc)] = P.dma(
                                "scalar", qa,
                                rsi_d[r0:r0 + 128, 512 * fc:512 * (fc + 1)],
                                obuf[sq_][:], waits=[(ss, eev)])
                            ev_rsw[("st", sq_)] = ev_rsw[(ci, sq_, fc)]
                    ev_ow[oi] = mm_ev
                    oi += 1
                for g in (2 * half, 2 * half + 1):
                    w_ = [(qa, ev_rsw[(cgrp[1], 3, 7)])]
                    rs_done[g] = P.op(
                        "gpsimd", lambda gp, g=g: gp.collective_compute(
                            "ReduceScatter", ALU.add, replica_groups=pairs,
                            ins=[rsi_d[512 * g:512 * (g + 1), :].opt()],
                            outs=[rso_d[256 * g:256 * (g + 1), :].opt()]),
                        waits=w_, inc=ccol)
            P.emit_phase(block)

        with ExitStack() as s3:
            a_sb = sb(s3, "a_sb", [128, 8 * DM], BF16)
            bias_sb = sb(s3, "bias_sb", [128, DM], F32)
            ssum = sb(s3, "ssum", [128, 4 * 16], F32)
            ssq = sb(s3, "ssq", [128, 4 * 16], F32)
            stat2 = sb(s3, "stat2", [128, 8], F32)

            bar2 = P.barrier_waits()
            ev_bias = P.dma("sync", qs, bias_sb[:], biasr_in[:], waits=bar2)

            ev_a = {}
            if True:
                osum = [sb(s3, f"osum{i}", [128, DM], BF16) for i in range(2)]
                xres = sb(s3, "xres_t", [128, DM], F32)
                r1 = sb(s3, "r1", [128, DM], F32)
                ev_r1 = {}
                ev_sqd = {}
                ev_nrm = {}
                for tt in range(8):
                    g = tt // 2
                    bb = tt % 2
                    w = [(ccol, rs_done[g])] + list(bar2)
                    if tt >= 2:
                        w.append((ss, ev_sqd[tt - 2]))
                    e_os = P.dma("sync", qs, osum[bb][:],
                                 rso_d[128 * tt:128 * (tt + 1), :], waits=w)
                    w2 = [(vs, ev_r1[tt - 1])] if tt >= 1 else []
                    e_xr = P.dma("sync", qs, xres[:],
                                 xres_in[128 * tt:128 * (tt + 1), :], waits=w2)
                    rw = [(qs, e_xr), (qs, e_os)]
                    if tt >= 1:
                        rw.append((vs, ev_nrm[tt - 1]))
                        rw.append((ss, ev_sqd[tt - 1]))
                    ev_r1[tt] = P.op("vector", lambda v, bb=bb:
                                     v.scalar_tensor_tensor(
                                         r1[:], osum[bb][:], 1.0,
                                         xres[:], ALU.mult, ALU.add,
                                         accum_out=stat[:, 8 * bb:8 * bb + 1]),
                                     waits=rw, inc=vs, sdep=None)
                    ev_sqd[tt] = P.op("scalar", lambda sc, bb=bb:
                                      sc.activation(
                                          osum[bb][:], r1[:], AF.Square,
                                          accum_out=stat[:, 8 * bb + 1:
                                                         8 * bb + 2]),
                                      waits=[(vs, ev_r1[tt])], inc=ss,
                                      sdep=None)
                    P.op("vector", lambda v, bb=bb: v.tensor_scalar(
                        stat[:, 8 * bb + 2:8 * bb + 3],
                        stat[:, 8 * bb:8 * bb + 1], 1.0 / DM, None, ALU.mult),
                        sdep=ev_r1[tt])
                    P.op("vector", lambda v, bb=bb: v.tensor_scalar(
                        stat[:, 8 * bb + 3:8 * bb + 4],
                        stat[:, 8 * bb + 1:8 * bb + 2], 1.0 / DM, None,
                        ALU.mult), waits=[(ss, ev_sqd[tt])], sdep=None)
                    P.op("vector", lambda v, bb=bb: v.tensor_tensor(
                        stat[:, 8 * bb + 4:8 * bb + 5],
                        stat[:, 8 * bb + 2:8 * bb + 3],
                        stat[:, 8 * bb + 2:8 * bb + 3], ALU.mult))
                    ev_var = P.op("vector", lambda v, bb=bb: v.tensor_tensor(
                        stat[:, 8 * bb + 5:8 * bb + 6],
                        stat[:, 8 * bb + 3:8 * bb + 4],
                        stat[:, 8 * bb + 4:8 * bb + 5], ALU.subtract), inc=vs)
                    P.op("scalar", lambda sc, bb=bb: sc.activation(
                        stat[:, 8 * bb + 6:8 * bb + 7],
                        stat[:, 8 * bb + 5:8 * bb + 6], AF.Ln,
                        bias=beps[:, 0:1]),
                        waits=[(vs, ev_var)], sdep=None)
                    ev_rstd = P.op("scalar", lambda sc, bb=bb: sc.activation(
                        stat[:, 8 * bb + 7:8 * bb + 8],
                        stat[:, 8 * bb + 6:8 * bb + 7], AF.Exp, scale=-0.5),
                        inc=ss)
                    ev_nrm[tt] = P.op(
                        "vector", lambda v, tt=tt, bb=bb: v.tensor_scalar(
                            a_sb[:, DM * tt:DM * (tt + 1)], r1[:],
                            stat[:, 8 * bb + 2:8 * bb + 3],
                            stat[:, 8 * bb + 7:8 * bb + 8],
                            ALU.subtract, ALU.mult),
                        waits=[(ss, ev_rstd)], inc=vs, sdep=None)
                    ev_a[tt] = ev_nrm[tt]
            P4_DONE = (vs, ev_a[7])
            if dbg:
                evd = P.dma("sync", qs, dbg1_d[:], a_sb[:], waits=[P4_DONE])
                P4_DONE = (qs, evd)

            if True:
                aT = sb(s3, "aT", [128, 32 * 512], BF16)
                wsl = [sb(s3, f"ws{i}", [128, 32 * 256], BF16)
                       for i in range(2)]
                tbuf = sb(s3, "tbuf", [128, 256], F32)
                j256 = sb(s3, "j256", [128, 256], F32)
                outb = [sb(s3, f"outb{i}", [128, DM // 4], F32) for i in range(2)]

                last_out = None
                last_out_ev = {}
                ev_wsl = {}
                ev_fev = {}
                ev_sqf = {}
                ev_trev = None
                wi = 0
                for grp in range(2):
                    for it in range(32):
                        ev_tr = None
                        for tl in range(4):
                            tt = 4 * grp + tl
                            w = [P4_DONE] if dbg else \
                                [(vs, ev_a[4 * grp + 3])]
                            if ev_trev is not None:
                                w.append((ss, ev_trev))
                            ev_tr = P.op(
                                "tensor", lambda t, it=it, tl=tl, tt=tt:
                                t.transpose(PBt[:, 128 * tl:128 * (tl + 1)],
                                            a_sb[:, DM * tt + 128 * it:
                                                 DM * tt + 128 * (it + 1)],
                                            ident[:]),
                                waits=w if tl == 0 else (),
                                inc=ts if tl == 3 else None)
                        ev_trev = P.op("scalar", lambda sc, it=it:
                                       sc.copy(aT[:, 512 * it:512 * (it + 1)],
                                               PBt[:, 0:512]),
                                       waits=[(ts, ev_tr)], inc=ss)
                    AT_RDY = (ss, ev_trev)
                    if dbg and grp == 0:
                        evd2 = P.dma("sync", qs, dbg2_d[:], aT[:],
                                     waits=[AT_RDY])
                        AT_RDY = (qs, evd2)
                    for oc in range(16):
                        bw = oc % 2
                        w = [AT_RDY, (qa, dq_ev_st[63])]
                        if wi >= 2:
                            w.append((ts, ev_wsl[wi - 2]))
                        e_w = P.dma("sync", qs,
                                    wsl[bw].rearrange("p (k w) -> p k w",
                                                      w=256),
                                    wdq_d[oc], waits=w)
                        mm_ev = None
                        for k in range(32):
                            for sq_ in range(4):
                                bank = PB[sq_]
                                mw = []
                                if k == 0:
                                    mw = [(qs, e_w), AT_RDY]
                                    pr = ev_fev.get((wi - 1, sq_))
                                    if pr is not None:
                                        mw.append((vs, pr))
                                def f(t, bank=bank, k=k, sq_=sq_, bw=bw):
                                    wv_ = wsl[bw].rearrange(
                                        "p (kk w) -> p kk w", w=256)
                                    return t.matmul(
                                        bank[:, 0:256],
                                        aT[:, 512 * k + 128 * sq_:
                                           512 * k + 128 * (sq_ + 1)],
                                        wv_[:, k, :],
                                        start=(k == 0), stop=(k == 31))
                                mm_ev = P.op(
                                    "tensor", f, waits=mw,
                                    inc=ts if (k == 31 and sq_ == 3) else None)
                        ev_wsl[wi] = mm_ev
                        for sq_ in range(4):
                            tt = 4 * grp + sq_
                            asl = a_sb[:, DM * tt + 256 * oc:
                                       DM * tt + 256 * (oc + 1)]
                            P.op("vector", lambda v, sq_=sq_, oc=oc:
                                 v.tensor_tensor(
                                     tbuf[:], PB[sq_][:, 0:256],
                                     bias_sb[:, 256 * oc:256 * (oc + 1)],
                                     ALU.add),
                                 waits=[(ts, mm_ev), (qs, ev_bias)],
                                 sdep=None)
                            ev_res = P.op(
                                "vector", lambda v, asl=asl, sq_=sq_, oc=oc:
                                v.scalar_tensor_tensor(
                                    asl, tbuf[:], 1.0, asl, ALU.mult,
                                    ALU.add,
                                    accum_out=ssum[:, 16 * sq_ + oc:
                                                   16 * sq_ + oc + 1]),
                                inc=vs)
                            ev_fev[(wi, sq_)] = ev_res
                            ev_sqf[(wi, sq_)] = P.op(
                                "scalar", lambda sc, asl=asl, sq_=sq_, oc=oc:
                                sc.activation(
                                    j256[:], asl, AF.Square,
                                    accum_out=ssq[:, 16 * sq_ + oc:
                                                  16 * sq_ + oc + 1]),
                                waits=[(vs, ev_res)], inc=ss, sdep=None)
                        wi += 1
                    for sq_ in range(4):
                        tt = 4 * grp + sq_
                        w = [(vs, ev_fev[(wi - 1, sq_)])]
                        P.op("vector", lambda v, sq_=sq_: v.reduce_sum(
                            stat2[:, 0:1], ssum[:, 16 * sq_:16 * (sq_ + 1)],
                            AX), waits=w)
                        P.op("vector", lambda v, sq_=sq_: v.reduce_sum(
                            stat2[:, 1:2], ssq[:, 16 * sq_:16 * (sq_ + 1)],
                            AX), waits=[(ss, ev_sqf[(wi - 1, sq_)])])
                        P.op("vector", lambda v: v.tensor_scalar(
                            stat2[:, 2:3], stat2[:, 0:1], 1.0 / DM, None,
                            ALU.mult))
                        P.op("vector", lambda v: v.tensor_scalar(
                            stat2[:, 3:4], stat2[:, 1:2], 1.0 / DM, None,
                            ALU.mult))
                        P.op("vector", lambda v: v.tensor_tensor(
                            stat2[:, 4:5], stat2[:, 2:3], stat2[:, 2:3],
                            ALU.mult))
                        ev_var = P.op("vector", lambda v: v.tensor_tensor(
                            stat2[:, 5:6], stat2[:, 3:4], stat2[:, 4:5],
                            ALU.subtract), inc=vs)
                        P.op("scalar", lambda sc: sc.activation(
                            stat2[:, 6:7], stat2[:, 5:6], AF.Ln, bias=beps[:, 0:1]),
                            waits=[(vs, ev_var)])
                        ev_rstd = P.op("scalar", lambda sc: sc.activation(
                            stat2[:, 7:8], stat2[:, 6:7], AF.Exp, scale=-0.5),
                            inc=ss)
                        for hf in range(4):
                            bo2 = (4 * tt + hf) % 2
                            ob = outb[bo2]
                            ow = [(ss, ev_rstd)]
                            if last_out_ev.get(bo2) is not None:
                                ow.append((qa, last_out_ev[bo2]))
                            ev_o = P.op(
                                "vector", lambda v, tt=tt, ob=ob, hf=hf:
                                v.tensor_scalar(
                                    ob[:],
                                    a_sb[:, DM * tt + 1024 * hf:
                                         DM * tt + 1024 * (hf + 1)],
                                    stat2[:, 2:3], stat2[:, 7:8],
                                    ALU.subtract, ALU.mult),
                                waits=ow, inc=vs, sdep=None)
                            last_out = P.dma(
                                "scalar", qa,
                                out_ext[128 * tt:128 * (tt + 1),
                                        1024 * hf:1024 * (hf + 1)],
                                ob[:], waits=[(vs, ev_o)])
                            last_out_ev[bo2] = last_out
                fin = qa.n
                fin_qs = qs.n
                P.op("scalar", lambda sc: sc.nop(),
                     waits=[(qa, fin), (qs, fin_qs)])
            P.emit_phase(block)

    return nc


def _rope_tables():
    inv_freq = 1.0 / (10000.0 ** (np.arange(0, D, 2, dtype=np.float64) / D))
    freqs = np.arange(S, dtype=np.float64)[:, None] * inv_freq[None, :]
    emb = np.concatenate([freqs, freqs], axis=-1)
    cos = np.ascontiguousarray(np.cos(emb).T.astype(np.float32))
    sin = np.sin(emb).T.astype(np.float32)
    sgn = np.ones((D, 1), np.float32)
    sgn[:64] = -1.0
    return cos, np.ascontiguousarray(sin * sgn)


def kernel(L0_attn_hidden_states, L0_attn_q_w, L0_attn_k_w, L0_attn_v_w,
           L0_attn_o_w, L0_ffn_weight_packed, L0_ffn_weight_scale,
           L0_ffn_weight_var_mask, L0_ffn_bias):
    x = np.asarray(L0_attn_hidden_states)
    qw = np.asarray(L0_attn_q_w)
    kw = np.asarray(L0_attn_k_w)
    vw = np.asarray(L0_attn_v_w)
    ow = np.asarray(L0_attn_o_w)
    pk = np.asarray(L0_ffn_weight_packed)
    sc = np.asarray(L0_ffn_weight_scale)
    mk = np.asarray(L0_ffn_weight_var_mask)
    bias = np.asarray(L0_ffn_bias)

    perm = np.concatenate([np.arange(0, DM, 2), np.arange(1, DM, 2)])
    cos, sin = _rope_tables()

    p16 = np.ascontiguousarray(pk.view(np.int16)[:, 0::2])
    p16 = p16.reshape(DM, 128, 16).transpose(1, 2, 0).reshape(2048, DM)
    pkT = np.ascontiguousarray(p16[:, perm])
    scP = np.ascontiguousarray(np.repeat(sc.reshape(DM, 128)[perm].T, 16,
                                         axis=0))
    mkP = np.ascontiguousarray(np.repeat(mk.reshape(DM, 128)[perm].T
                                         .view(np.uint8), 16, axis=0))
    biasr = np.ascontiguousarray(np.broadcast_to(bias[perm][None, :],
                                                 (128, DM)))

    in_maps = []
    for c in range(8):
        b, hh = c // 2, c % 2
        rows = slice(2048 * hh, 2048 * (hh + 1))
        in_maps.append({
            "xT": np.ascontiguousarray(x[b].T),
            "xres": np.ascontiguousarray(
                x[b, 1024 * hh:1024 * (hh + 1)][:, perm]),
            "qwT": np.ascontiguousarray(qw[rows].T),
            "kwT": np.ascontiguousarray(kw[rows].T),
            "vwT": np.ascontiguousarray(vw[rows].T),
            "owT": np.ascontiguousarray(ow[perm][:, rows].T),
            "pkT": pkT, "scP": scP, "mkP": mkP, "biasr": biasr,
            "cosD": cos, "sinD": sin,
            "flag": np.array([[1.0 if c < 2 else 0.0]], np.float32),
        })

    nc = build()
    res = run_bass_kernel_spmd(nc, in_maps, core_ids=list(range(8)))

    out = np.empty((4, S, DM), np.float32)
    inv = np.empty(DM, np.int64)
    inv[perm] = np.arange(DM)
    for c in range(8):
        b, hh = c // 2, c % 2
        out[b, 1024 * hh:1024 * (hh + 1)] = res.results[c]["out"][:, inv]
    return out
